# version 07
# baseline (speedup 1.0000x reference)
import contextlib
import numpy as np
import ml_dtypes
import concourse.bass as bass
import concourse.mybir as mybir
from concourse.bass_utils import run_bass_kernel_spmd

F32 = mybir.dt.float32
BF16 = mybir.dt.bfloat16
I32 = mybir.dt.int32
AF = mybir.ActivationFunctionType
ALU = mybir.AluOpType
AX = mybir.AxisListType.X
BF = ml_dtypes.bfloat16

NC = 8
S = 8192
D = 2048
T = S // NC
NH = 16

NOPE, ROPE, VD = 128, 64, 128
QKD = NOPE + ROPE
Q_LORA, KV_LORA = 768, 512
L0_SCALE = QKD ** -0.5
RMS_EPS = 1e-6
LN_EPS = 1e-5
INTER = 5504
O_PAD = 768
HD = 128
L1_SCALE = HD ** -0.5
DFF = 8192
RBIG = 12582912.0


class Sem:
    __slots__ = ("h", "n")

    def __init__(self, nc, name):
        self.h = nc.alloc_semaphore(name)
        self.n = 0

    def inc(self, inst, k=1):
        inst.then_inc(self.h, k)
        self.n += k
        return self.n


class Ctx:
    def __init__(self, nc):
        self.nc = nc
        self.sems = {}

    def sem(self, name):
        if name not in self.sems:
            self.sems[name] = Sem(self.nc, name)
        return self.sems[name]

    def wait(self, eng, sem, val=None):
        eng.wait_ge(sem.h, sem.n if val is None else val)

    def vflush(self, inst, tag="vfl"):
        s = self.sem(tag)
        s.inc(inst)
        self.nc.vector.wait_ge(s.h, s.n)

    def bar(self):
        self.nc.all_engine_barrier()


def cdiv(a, b):
    return (a + b - 1) // b


def sb(es, nc, name, shape, dtype):
    return es.enter_context(nc.sbuf_tensor(name, shape, dtype))


def psum(es, nc, name, shape, dtype=F32):
    return es.enter_context(nc.psum_tensor(name, shape, dtype))




def emit_ln(ctx, x_tiles, xlt, ident_bf, tag, dbg_stats=None):
    nc = ctx.nc
    se = ctx.sem("lnse_" + tag)
    ve = ctx.sem("lnve_" + tag)
    mm = ctx.sem("lnmm_" + tag)
    tev = ctx.sem("lntev_" + tag)
    with (
        nc.sbuf_tensor("lnstats_" + tag, [128, 32], F32) as stats,
        nc.sbuf_tensor("lnsq_" + tag, [128, D], F32) as sq,
        nc.sbuf_tensor("lnxln_" + tag, [128, D], BF16) as xln,
        nc.psum_tensor("lntp0_" + tag, [128, 512], BF16) as tp0,
        nc.psum_tensor("lntp1_" + tag, [128, 512], BF16) as tp1,
    ):
        tps = [tp0, tp1]
        ve.inc(nc.vector.memset(stats[:, 8:16], 0.0))
        ctx.wait(nc.scalar, ve)
        for t in range(8):
            nc.vector.reduce_sum(stats[:, t:t + 1], x_tiles[t][:, :], AX)
            se.inc(nc.scalar.activation(sq[:, :], x_tiles[t][:, :], AF.Square,
                                        accum_out=stats[:, 8 + t:9 + t]))
        ctx.wait(nc.vector, se)
        fl = "lnfl_" + tag
        ctx.vflush(nc.vector.tensor_scalar(stats[:, 16:24], stats[:, 0:8], 1.0 / D, None, ALU.mult), fl)
        ctx.vflush(nc.vector.tensor_tensor(stats[:, 24:32], stats[:, 16:24], stats[:, 16:24], ALU.mult), fl)
        ctx.vflush(nc.vector.tensor_scalar(stats[:, 8:16], stats[:, 8:16], 1.0 / D, None, ALU.mult), fl)
        ctx.vflush(nc.vector.tensor_sub(stats[:, 24:32], stats[:, 8:16], stats[:, 24:32]), fl)
        ve.inc(nc.vector.tensor_scalar(stats[:, 24:32], stats[:, 24:32], LN_EPS, None, ALU.add))
        ctx.wait(nc.scalar, ve)
        se.inc(nc.scalar.activation(stats[:, 24:32], stats[:, 24:32], AF.Sqrt))
        ctx.wait(nc.vector, se)
        ctx.vflush(nc.vector.reciprocal(stats[:, 24:32], stats[:, 24:32]), fl)
        ve.inc(nc.vector.nop())
        if dbg_stats is not None:
            ctx.wait(nc.gpsimd, ve)
            dsem = ctx.sem("lndbg_" + tag)
            dsem.inc(nc.gpsimd.dma_start(out=dbg_stats, in_=stats[:, :]), 16)
            ctx.wait(nc.gpsimd, dsem)

        ng = 0
        for t in range(8):
            if t > 0:
                ctx.wait(nc.vector, mm, ng)
            ve.inc(nc.vector.tensor_scalar(
                xln[:, :], x_tiles[t][:, :], stats[:, 16 + t:17 + t],
                stats[:, 24 + t:25 + t], ALU.subtract, ALU.mult))
            ctx.wait(nc.tensor, ve)
            for g in range(4):
                b = ng % 2
                if ng >= 2:
                    ctx.wait(nc.tensor, tev, ng - 1)
                for c4 in range(4):
                    c = g * 4 + c4
                    inst = nc.tensor.transpose(tps[b][:, c4 * 128:(c4 + 1) * 128],
                                               xln[:, c * 128:(c + 1) * 128],
                                               ident_bf[:, :])
                mm.inc(inst)
                ctx.wait(nc.vector, mm)
                for c4 in range(4):
                    c = g * 4 + c4
                    inst = nc.vector.tensor_copy(
                        xlt[:, c * T + t * 128:c * T + (t + 1) * 128],
                        tps[b][:, c4 * 128:(c4 + 1) * 128])
                tev.inc(inst)
                ng += 1
    ctx.bar()



def emit_stream_mm(ctx, tag, slab_pieces, K, M, rhs_fn, out_cb, wbufs, psum,
                   n_free, free_tile=512):
    nc = ctx.nc
    nk = cdiv(K, 128)
    nm = cdiv(M, 128)
    nf = cdiv(n_free, free_tile)
    wld = ctx.sem("wld_" + tag)
    wfree = ctx.sem("wfree_" + tag)
    mmd = ctx.sem("mmd_" + tag)
    oev = ctx.sem("oev_" + tag)
    for m in range(min(2, nm)):
        for ap, row0, nrows, col0, mw in slab_pieces(m):
            wld.inc(nc.sync.dma_start(out=wbufs[m % 2][row0:row0 + nrows, col0:col0 + mw],
                                      in_=ap), 16)
    npc = len(slab_pieces(0))
    npc_ev = None
    g = 0
    for m in range(nm):
        buf = wbufs[m % 2]
        ctx.wait(nc.tensor, wld, 16 * npc * (m + 1))
        for f in range(nf):
            f0 = f * free_tile
            fw = min(free_tile, n_free - f0)
            ps = psum[g % len(psum)]
            if g >= len(psum):
                ctx.wait(nc.tensor, oev, npc_ev * (g - len(psum) + 1))
            mw = min(128, M - m * 128)
            for kc in range(nk):
                kk = min(128, K - kc * 128)
                inst = nc.tensor.matmul(ps[0:mw, 0:fw], buf[0:kk, kc * 128:kc * 128 + mw],
                                        rhs_fn(kc, kk, f0, fw),
                                        start=(kc == 0), stop=(kc == nk - 1))
            mmd.inc(inst)
            insts = out_cb(m, f0, fw, ps, mmd.n)
            if not isinstance(insts, (list, tuple)):
                insts = [insts]
            for ii in insts:
                oev.inc(ii)
            if npc_ev is None:
                npc_ev = len(insts)
            g += 1
        if m + 2 < nm:
            ctx.wait(nc.sync, mmd, nf * (m + 1))
            for ap, row0, nrows, col0, mw in slab_pieces(m + 2):
                wld.inc(nc.sync.dma_start(out=wbufs[m % 2][row0:row0 + nrows, col0:col0 + mw],
                                          in_=ap), 16)
    ctx.bar()



def emit_token_proj(ctx, tag, lhs_fn, nkc, w_tiles, x_tiles, extra_rhs=None):
    nc = ctx.nc
    mm = ctx.sem("tpmm_" + tag)
    ve = ctx.sem("tpve_" + tag)
    with (
        nc.psum_tensor("tpps0_" + tag, [128, 512], F32) as ps0,
        nc.psum_tensor("tpps1_" + tag, [128, 512], F32) as ps1,
    ):
        pss = [ps0, ps1]
        g = 0
        for tg in range(8):
            for dc in range(4):
                ps = pss[g % 2]
                if g >= 2:
                    ctx.wait(nc.tensor, ve, g - 1)
                for kc in range(nkc):
                    inst = nc.tensor.matmul(ps[:, :], lhs_fn(kc, tg),
                                            w_tiles[kc][:, dc * 512:(dc + 1) * 512],
                                            start=(kc == 0),
                                            stop=(kc == nkc - 1 and extra_rhs is None))
                if extra_rhs is not None:
                    ones_row, brow_fn = extra_rhs
                    inst = nc.tensor.matmul(ps[:, :], ones_row,
                                            brow_fn(dc * 512, 512),
                                            start=False, stop=True)
                mm.inc(inst)
                ctx.wait(nc.vector, mm)
                ve.inc(nc.vector.scalar_tensor_tensor(
                    x_tiles[tg][:, dc * 512:(dc + 1) * 512], ps[:, :], 1.0,
                    x_tiles[tg][:, dc * 512:(dc + 1) * 512], ALU.mult, ALU.add))
                g += 1
    ctx.bar()



def emit_rms(ctx, es_unused, tag, src_big, nk, nreal, g_ap, scale, out_big, ones_bf,
             ones_f32r):
    nc = ctx.nc
    se = ctx.sem("rmse_" + tag)
    ve = ctx.sem("rmve_" + tag)
    mm = ctx.sem("rmmm_" + tag)
    with contextlib.ExitStack() as es2:
        sqb = sb(es2, nc, "rmsq_" + tag, [128, nk * T], BF16)
        row = sb(es2, nc, "rmrow_" + tag, [1, T], F32)
        rstdB = sb(es2, nc, "rmbc_" + tag, [128, T], BF16)
        tmp = sb(es2, nc, "rmtmp_" + tag, [128, T], BF16)
        ps = psum(es2, nc, "rmps_" + tag, [1, 512])
        ps2 = psum(es2, nc, "rmps2_" + tag, [128, 512])
        se.inc(nc.scalar.activation(sqb[:, :], src_big[:, :], AF.Square))
        ctx.wait(nc.tensor, se)
        for half in range(2):
            for kc in range(nk):
                inst = nc.tensor.matmul(ps[0:1, :], ones_bf[:, 0:1],
                                        sqb[:, kc * T + half * 512:kc * T + half * 512 + 512],
                                        start=(kc == 0), stop=(kc == nk - 1))
            mm.inc(inst)
            ctx.wait(nc.vector, mm)
            ve.inc(nc.vector.tensor_copy(row[0:1, half * 512:half * 512 + 512], ps[0:1, :]))
            ctx.wait(nc.tensor, ve)
        fl = "rmfl_" + tag
        ctx.vflush(nc.vector.tensor_scalar(row[0:1, :], row[0:1, :], 1.0 / nreal,
                                           RMS_EPS, ALU.mult, ALU.add), fl)
        ve.inc(nc.vector.nop())
        ctx.wait(nc.scalar, ve)
        se.inc(nc.scalar.activation(row[0:1, :], row[0:1, :], AF.Sqrt))
        ctx.wait(nc.vector, se)
        ctx.vflush(nc.vector.reciprocal(row[0:1, :], row[0:1, :]), fl)
        ve.inc(nc.vector.nop())
        ctx.wait(nc.tensor, ve)
        for half in range(2):
            mm.inc(nc.tensor.matmul(ps2[:, :], ones_f32r[0:1, :],
                                    row[0:1, half * 512:half * 512 + 512],
                                    start=True, stop=True))
            ctx.wait(nc.vector, mm)
            ve.inc(nc.vector.tensor_copy(rstdB[:, half * 512:half * 512 + 512], ps2[:, :]))
            ctx.wait(nc.tensor, ve)
        for kc in range(nk):
            nc.vector.tensor_scalar(tmp[:, :], src_big[:, kc * T:(kc + 1) * T],
                                    g_ap[:, kc:kc + 1], float(scale), ALU.mult, ALU.mult)
            ve.inc(nc.vector.tensor_tensor(out_big[:, kc * T:(kc + 1) * T],
                                           tmp[:, :], rstdB[:, :], ALU.mult))
    ctx.bar()



def emit_decode(ctx, es, tag, idx_in, rows, cols, lutb1, ident_bf, stage_out,
                pad_row0=None, pad_col0=None, dec_out=None):
    nc = ctx.nc
    ld = ctx.sem("dcld_" + tag)
    ve = ctx.sem("dcve_" + tag)
    mm = ctx.sem("dcmm_" + tag)
    tse = ctx.sem("dcts_" + tag)
    st = ctx.sem("dcst_" + tag)
    idxt = sb(es, nc, "dcidx_" + tag, [128, cols], I32)
    idxb = sb(es, nc, "dcib_" + tag, [128, cols], BF16)
    acc = sb(es, nc, "dcacc_" + tag, [128, cols], BF16)
    tj = sb(es, nc, "dctj_" + tag, [128, cols], BF16)
    qtr = sb(es, nc, "dctr_" + tag, [128, cols], BF16)
    with contextlib.ExitStack() as es2:
        tpb = [psum(es2, nc, "dctp0_" + tag, [128, 512], BF16),
               psum(es2, nc, "dctp1_" + tag, [128, 512], BF16)]
        tb = 0
        for rt in range(rows // 128):
            if rt > 0:
                ctx.wait(nc.sync, st)
            ld.inc(nc.sync.dma_start(out=idxt[:, :],
                                     in_=idx_in[rt * 128:(rt + 1) * 128, :]), 16)
            ctx.wait(nc.vector, ld)
            ctx.wait(nc.vector, st)
            nc.vector.tensor_copy(idxb[:, :], idxt[:, :])
            for j in range(16):
                if j == 0:
                    inst = nc.vector.tensor_scalar(acc[:, :], idxb[:, :], 0.0,
                                                   lutb1[:, 0:1], ALU.is_equal, ALU.mult)
                else:
                    nc.vector.tensor_scalar(tj[:, :], idxb[:, :], float(j),
                                            lutb1[:, j:j + 1], ALU.is_equal, ALU.mult)
                    inst = nc.vector.tensor_add(acc[:, :], acc[:, :], tj[:, :])
            ve.inc(inst)
            if pad_col0 is not None:
                ve.inc(nc.vector.memset(acc[:, pad_col0:cols], 0.0))
            if dec_out is not None:
                ctx.wait(nc.gpsimd, ve)
                st.inc(nc.gpsimd.dma_start(out=dec_out[rt * 128:(rt + 1) * 128, :],
                                           in_=acc[:, :]), 16)
            ctx.wait(nc.tensor, ve)
            ncch = cols // 128
            for g in range(cdiv(ncch, 4)):
                n4 = min(4, ncch - g * 4)
                tp = tpb[tb % 2]
                if tb >= 2:
                    ctx.wait(nc.tensor, tse, tb - 1)
                for c4 in range(n4):
                    c = g * 4 + c4
                    inst = nc.tensor.transpose(tp[:, c4 * 128:(c4 + 1) * 128],
                                               acc[:, c * 128:(c + 1) * 128],
                                               ident_bf[:, :])
                mm.inc(inst)
                ctx.wait(nc.vector, mm)
                for c4 in range(n4):
                    c = g * 4 + c4
                    inst = nc.vector.tensor_copy(qtr[:, c * 128:(c + 1) * 128],
                                                 tp[:, c4 * 128:(c4 + 1) * 128])
                tse.inc(inst)
                tb += 1
            if pad_row0 is not None and rt == rows // 128 - 1:
                j0 = pad_row0 - rt * 128
                for c in range(cols // 128):
                    inst = nc.vector.memset(qtr[:, c * 128 + j0:(c + 1) * 128], 0.0)
                ve.inc(inst)
                ctx.wait(nc.gpsimd, ve)
            ctx.wait(nc.gpsimd, tse)
            for c in range(cols // 128):
                st.inc(nc.gpsimd.dma_start(
                    out=stage_out[c * 128:(c + 1) * 128, rt * 128:(rt + 1) * 128],
                    in_=qtr[:, c * 128:(c + 1) * 128]), 16)
    ctx.wait(nc.gpsimd, st)
    ctx.bar()



def plain_pieces(w_ap, K):
    nk = cdiv(K, 128)
    M = w_ap.shape[1]
    def pieces(m):
        mw = min(128, M - m * 128)
        return [(w_ap[kc * 128:min((kc + 1) * 128, K), m * 128:m * 128 + mw],
                 0, min(128, K - kc * 128), kc * 128, mw) for kc in range(nk)]
    return pieces


def build_A(parts=("qkv", "decode", "absum")):
    nc = bass.Bass("TRN2", target_bir_lowering=False)
    ctx = Ctx(nc)
    x_in = nc.dram_tensor("x_in", [T, D], F32, kind="ExternalInput").ap()
    ident_in = nc.dram_tensor("ident", [128, 128], BF16, kind="ExternalInput").ap()
    wdq_f = nc.dram_tensor("wdq_f", [D, Q_LORA], F32, kind="ExternalInput").ap()
    wuq_f = nc.dram_tensor("wuq_f", [Q_LORA, 4096], F32, kind="ExternalInput").ap()
    wdkv_f = nc.dram_tensor("wdkv_f", [D, KV_LORA], F32, kind="ExternalInput").ap()
    wkr_f = nc.dram_tensor("wkr_f", [D, ROPE], F32, kind="ExternalInput").ap()
    wuk_f = nc.dram_tensor("wuk_f", [KV_LORA, D], F32, kind="ExternalInput").ap()
    wuv_f = nc.dram_tensor("wuv_f", [KV_LORA, D], F32, kind="ExternalInput").ap()
    gq_v = nc.dram_tensor("gq_v", [128, 6], F32, kind="ExternalInput").ap()
    gkv_v = nc.dram_tensor("gkv_v", [128, 4], F32, kind="ExternalInput").ap()
    cos0 = nc.dram_tensor("cos0", [32, T], BF16, kind="ExternalInput").ap()
    sin0 = nc.dram_tensor("sin0", [32, T], BF16, kind="ExternalInput").ap()
    idxg = nc.dram_tensor("idxg", [O_PAD, D], I32, kind="ExternalInput").ap()
    idxu = nc.dram_tensor("idxu", [O_PAD, D], I32, kind="ExternalInput").ap()
    idxd = nc.dram_tensor("idxd", [D, O_PAD], I32, kind="ExternalInput").ap()
    lut_in = nc.dram_tensor("lut_in", [1, 16], F32, kind="ExternalInput").ap()
    w1s = nc.dram_tensor("w1s", [DFF // NC, D], F32, kind="ExternalInput").ap()
    w2s = nc.dram_tensor("w2s", [D, DFF // NC], F32, kind="ExternalInput").ap()
    cast_slices = {}
    for nm in ("wo0", "wq1", "wk1", "wv1", "wo1"):
        cast_slices[nm] = nc.dram_tensor(nm + "_s", [D // NC, D], F32, kind="ExternalInput").ap()

    qn_st = nc.dram_tensor("qn_st", [NH * NOPE, T], BF16, kind="ExternalOutput").ap()
    qr_st = nc.dram_tensor("qr_st", [NH * ROPE, T], BF16, kind="ExternalOutput").ap()
    kn_st = nc.dram_tensor("kn_st", [NH * NOPE, T], BF16, kind="ExternalOutput").ap()
    vt_st = nc.dram_tensor("vt_st", [T, D], BF16, kind="ExternalOutput").ap()
    kpe_st = nc.dram_tensor("kpe_st", [ROPE, T], BF16, kind="ExternalOutput").ap()
    gate_T = nc.dram_tensor("gate_T", [D, O_PAD], BF16, kind="ExternalOutput").ap()
    up_T = nc.dram_tensor("up_T", [D, O_PAD], BF16, kind="ExternalOutput").ap()
    down_T = nc.dram_tensor("down_T", [O_PAD, D], BF16, kind="ExternalOutput").ap()
    absum = nc.dram_tensor("absum", [1, 2], F32, kind="ExternalOutput").ap()
    cast_outs = {}
    for nm in ("wo0", "wq1", "wk1", "wv1", "wo1"):
        cast_outs[nm] = nc.dram_tensor(nm + "_bf", [D // NC, D], BF16, kind="ExternalOutput").ap()

    wdq_b = nc.dram_tensor("wdq_b", [D, Q_LORA], BF16)
    wuq_b = nc.dram_tensor("wuq_b", [Q_LORA, 4096], BF16)
    wdkv_b = nc.dram_tensor("wdkv_b", [D, KV_LORA], BF16)
    wkr_b = nc.dram_tensor("wkr_b", [D, ROPE], BF16)
    wuk_b = nc.dram_tensor("wuk_b", [KV_LORA, D], BF16)
    wuv_b = nc.dram_tensor("wuv_b", [KV_LORA, D], BF16)

    with contextlib.ExitStack() as es:
        es.enter_context(nc.Block())
        cld = ctx.sem("cld")
        for dst, src_ap in ((wdq_b, wdq_f), (wuq_b, wuq_f), (wdkv_b, wdkv_f),
                            (wkr_b, wkr_f), (wuk_b, wuk_f), (wuv_b, wuv_f)):
            cld.inc(nc.gpsimd.dma_start(out=dst.ap(), in_=src_ap), 16)
        for nm in ("wo0", "wq1", "wk1", "wv1", "wo1"):
            cld.inc(nc.gpsimd.dma_start(out=cast_outs[nm], in_=cast_slices[nm]), 16)

        with contextlib.ExitStack() as es1:
            xlt = sb(es, nc, "xlt", [128, 16 * T], BF16)
            identb = sb(es, nc, "identb", [128, 128], BF16)
            onesb = sb(es, nc, "onesb", [128, 1], BF16)
            onesf = sb(es, nc, "onesf", [1, 128], F32)
            gq_sb = sb(es, nc, "gq_sb", [128, 6], F32)
            gkv_sb = sb(es, nc, "gkv_sb", [128, 4], F32)
            lutb1 = sb(es, nc, "lutb1", [128, 16], F32)
            cos_sb = sb(es, nc, "cos_sb", [32, T], BF16)
            sin_sb = sb(es, nc, "sin_sb", [32, T], BF16)
            ld0 = ctx.sem("ld0")
            with contextlib.ExitStack() as esx:
                x_tiles = [sb(esx, nc, "x%d" % t, [128, D], F32) for t in range(8)]
                for t in range(8):
                    ld0.inc(nc.sync.dma_start(out=x_tiles[t][:, :],
                                              in_=x_in[t * 128:(t + 1) * 128, :]), 16)
                ld0.inc(nc.sync.dma_start(out=identb[:, :], in_=ident_in), 16)
                ld0.inc(nc.sync.dma_start(out=gq_sb[:, :], in_=gq_v), 16)
                ld0.inc(nc.sync.dma_start(out=gkv_sb[:, :], in_=gkv_v), 16)
                ld0.inc(nc.sync.dma_start(out=cos_sb[:, :], in_=cos0), 16)
                ld0.inc(nc.sync.dma_start(out=sin_sb[:, :], in_=sin0), 16)
                nc.vector.memset(onesb[:, :], 1.0)
                nc.vector.memset(onesf[:, :], 1.0)
                ctx.wait(nc.tensor, ld0)
                ctx.wait(nc.vector, ld0)
                ctx.wait(nc.scalar, ld0)
                ctx.bar()
                emit_ln(ctx, x_tiles, xlt, identb, "a_ln")
            lld = ctx.sem("lld")
            lrow = sb(es, nc, "lrow", [1, 16], F32)
            lld.inc(nc.sync.dma_start(out=lrow[:, :], in_=lut_in), 16)
            ctx.wait(nc.tensor, lld)
            with contextlib.ExitStack() as esl:
                lps = psum(esl, nc, "lps", [128, 16])
                lmm = ctx.sem("lmm")
                lmm.inc(nc.tensor.matmul(lps[:, :], onesf[0:1, :], lrow[0:1, :],
                                         start=True, stop=True))
                ctx.wait(nc.vector, lmm)
                ctx.vflush(nc.vector.tensor_copy(lutb1[:, :], lps[:, :]), "lfl")

            ctx.wait(nc.tensor, cld)
            ctx.bar()

            if "qkv" in parts:
              with contextlib.ExitStack() as esq:
                  qbig = sb(esq, nc, "qbig", [128, 32 * T], BF16)
                  with contextlib.ExitStack() as esq2:
                      qlr = sb(esq2, nc, "qlr", [128, 6 * T], BF16)
                      qn_b = sb(esq2, nc, "qn_b", [128, 6 * T], BF16)
                      wb0 = sb(esq2, nc, "qwb0", [128, 16 * 128], BF16)
                      wb1 = sb(esq2, nc, "qwb1", [128, 16 * 128], BF16)
                      with contextlib.ExitStack() as esp:
                          ps0 = psum(esp, nc, "qps0", [128, 512])
                          ps1 = psum(esp, nc, "qps1", [128, 512])

                          def qlr_cb(m, f0, fw, ps, mmn):
                              nc.vector.wait_ge(ctx.sem("mmd_a_qlr").h, mmn)
                              return nc.vector.tensor_copy(qlr[:, m * T + f0:m * T + f0 + fw],
                                                           ps[:, 0:fw])

                          emit_stream_mm(ctx, "a_qlr", plain_pieces(wdq_b.ap(), D), D, Q_LORA,
                                         lambda kc, kk, f0, fw: xlt[:, kc * T + f0:kc * T + f0 + fw],
                                         qlr_cb, [wb0, wb1], [ps0, ps1], n_free=T)
                      emit_rms(ctx, esq2, "a_qrms", qlr, 6, Q_LORA, gq_sb, L0_SCALE, qn_b,
                               onesb, onesf)

                      with contextlib.ExitStack() as esp:
                          ps0 = psum(esp, nc, "qps2", [128, 512])
                          ps1 = psum(esp, nc, "qps3", [128, 512])

                          def qeff_cb(m, f0, fw, ps, mmn):
                              nc.vector.wait_ge(ctx.sem("mmd_a_qe").h, mmn)
                              return nc.vector.tensor_copy(qbig[:, m * T + f0:m * T + f0 + fw],
                                                           ps[:, 0:fw])

                          emit_stream_mm(ctx, "a_qe", plain_pieces(wuq_b.ap(), Q_LORA),
                                         Q_LORA, 4096,
                                         lambda kc, kk, f0, fw: qn_b[:, kc * T + f0:kc * T + f0 + fw],
                                         qeff_cb, [wb0, wb1], [ps0, ps1], n_free=T)

                  o1t = sb(esq, nc, "o1t", [32, 16 * T], BF16)
                  x2c = sb(esq, nc, "x2c", [32, 16 * T], BF16)
                  rtmp = sb(esq, nc, "rtmp", [32, 16 * T], BF16)
                  x1 = qbig[0:32, :].rearrange("p (c t) -> p c t", t=T)[:, 1::2, :]
                  x2src = qbig[32:64, :].rearrange("p (c t) -> p c t", t=T)[:, 1::2, :]
                  x2v = x2c[:, :].rearrange("p (c t) -> p c t", t=T)
                  o1 = o1t[:, :].rearrange("p (c t) -> p c t", t=T)
                  tm = rtmp[:, :].rearrange("p (c t) -> p c t", t=T)
                  cosb = cos_sb[:, :].rearrange("p t -> p () t").broadcast_to([32, NH, T])
                  sinb = sin_sb[:, :].rearrange("p t -> p () t").broadcast_to([32, NH, T])
                  rve = ctx.sem("a_rve")
                  nc.vector.tensor_copy(x2v, x2src)
                  nc.vector.tensor_tensor(o1, x1, cosb, ALU.mult)
                  nc.vector.tensor_tensor(tm, x2v, sinb, ALU.mult)
                  nc.vector.tensor_sub(o1, o1, tm)
                  nc.vector.tensor_tensor(tm, x1, sinb, ALU.mult)
                  nc.vector.tensor_tensor(x2v, x2v, cosb, ALU.mult)
                  rve.inc(nc.vector.tensor_add(tm, tm, x2v))
                  ctx.wait(nc.gpsimd, rve)
                  stq = ctx.sem("a_stq")
                  for h in range(NH):
                      stq.inc(nc.gpsimd.dma_start(
                          out=qn_st[h * NOPE:(h + 1) * NOPE, :],
                          in_=qbig[:, (2 * h) * T:(2 * h + 1) * T]), 16)
                      stq.inc(nc.gpsimd.dma_start(
                          out=qr_st[h * ROPE:h * ROPE + 32, :],
                          in_=o1t[:, h * T:(h + 1) * T]), 16)
                      stq.inc(nc.gpsimd.dma_start(
                          out=qr_st[h * ROPE + 32:(h + 1) * ROPE, :],
                          in_=rtmp[:, h * T:(h + 1) * T]), 16)
                  ctx.wait(nc.gpsimd, stq)
              ctx.bar()

              with contextlib.ExitStack() as esk:
                  kvl = sb(esk, nc, "kvl", [128, 4 * T], BF16)
                  kvn = sb(esk, nc, "kvn", [128, 4 * T], BF16)
                  wb0 = sb(esk, nc, "kwb0", [128, 16 * 128], BF16)
                  wb1 = sb(esk, nc, "kwb1", [128, 16 * 128], BF16)
                  with contextlib.ExitStack() as esp:
                      ps0 = psum(esp, nc, "kps0", [128, 512])
                      ps1 = psum(esp, nc, "kps1", [128, 512])

                      def kv_cb(m, f0, fw, ps, mmn):
                          nc.vector.wait_ge(ctx.sem("mmd_a_kv").h, mmn)
                          return nc.vector.tensor_copy(kvl[:, m * T + f0:m * T + f0 + fw],
                                                       ps[:, 0:fw])

                      emit_stream_mm(ctx, "a_kv", plain_pieces(wdkv_b.ap(), D), D, KV_LORA,
                                     lambda kc, kk, f0, fw: xlt[:, kc * T + f0:kc * T + f0 + fw],
                                     kv_cb, [wb0, wb1], [ps0, ps1], n_free=T)
                  emit_rms(ctx, esk, "a_krms", kvl, 4, KV_LORA, gkv_sb, 1.0, kvn,
                           onesb, onesf)

                  knb = sb(esk, nc, "knb", [128, 16 * T], BF16)
                  with contextlib.ExitStack() as esp:
                      ps0 = psum(esp, nc, "kps2", [128, 512])
                      ps1 = psum(esp, nc, "kps3", [128, 512])

                      def kn_cb(m, f0, fw, ps, mmn):
                          nc.vector.wait_ge(ctx.sem("mmd_a_kn").h, mmn)
                          return nc.vector.tensor_copy(knb[:, m * T + f0:m * T + f0 + fw],
                                                       ps[:, 0:fw])

                      emit_stream_mm(ctx, "a_kn", plain_pieces(wuk_b.ap(), KV_LORA),
                                     KV_LORA, D,
                                     lambda kc, kk, f0, fw: kvn[:, kc * T + f0:kc * T + f0 + fw],
                                     kn_cb, [wb0, wb1], [ps0, ps1], n_free=T)
                  stk = ctx.sem("a_stk")
                  for c in range(16):
                      stk.inc(nc.gpsimd.dma_start(out=kn_st[c * 128:(c + 1) * 128, :],
                                                  in_=knb[:, c * T:(c + 1) * T]), 16)

                  wuv_sb = sb(esk, nc, "wuv_sb", [128, 4 * D], BF16)
                  vld = ctx.sem("a_vld")
                  for kc in range(4):
                      vld.inc(nc.sync.dma_start(out=wuv_sb[:, kc * D:(kc + 1) * D],
                                                in_=wuv_b.ap()[kc * 128:(kc + 1) * 128, :]), 16)
                  ctx.wait(nc.tensor, vld)
                  vrow = sb(esk, nc, "vrow", [128, D], BF16)
                  vmm = ctx.sem("a_vmm")
                  vve = ctx.sem("a_vve")
                  with contextlib.ExitStack() as esp:
                      vps = [psum(esp, nc, "vps%d" % i, [128, 512]) for i in range(2)]
                      for tt in range(8):
                          for dc in range(4):
                              gidx = tt * 4 + dc
                              ps = vps[gidx % 2]
                              if gidx >= 2:
                                  ctx.wait(nc.tensor, vve, gidx - 1)
                              for kc in range(4):
                                  inst = nc.tensor.matmul(
                                      ps[:, :], kvn[:, kc * T + tt * 128:kc * T + (tt + 1) * 128],
                                      wuv_sb[:, kc * D + dc * 512:kc * D + (dc + 1) * 512],
                                      start=(kc == 0), stop=(kc == 3))
                              vmm.inc(inst)
                              ctx.wait(nc.vector, vmm)
                              vve.inc(nc.vector.tensor_copy(vrow[:, dc * 512:(dc + 1) * 512],
                                                            ps[:, :]))
                          ctx.wait(nc.gpsimd, vve)
                          stk.inc(nc.gpsimd.dma_start(out=vt_st[tt * 128:(tt + 1) * 128, :],
                                                      in_=vrow[:, :]), 16)
                          ctx.wait(nc.vector, stk)
                  ctx.wait(nc.gpsimd, stk)

                  kpe = sb(esk, nc, "kpe", [64, T], BF16)
                  kper = sb(esk, nc, "kper", [64, T], BF16)
                  ktmp = sb(esk, nc, "ktmp", [32, T], BF16)
                  with contextlib.ExitStack() as esp:
                      ps0 = psum(esp, nc, "kps4", [64, 512])
                      ps1 = psum(esp, nc, "kps5", [64, 512])

                      def kpe_cb(m, f0, fw, ps, mmn):
                          nc.vector.wait_ge(ctx.sem("mmd_a_kp").h, mmn)
                          return nc.vector.tensor_copy(kpe[:, f0:f0 + fw], ps[0:64, 0:fw])

                      emit_stream_mm(ctx, "a_kp", plain_pieces(wkr_b.ap(), D), D, ROPE,
                                     lambda kc, kk, f0, fw: xlt[:, kc * T + f0:kc * T + f0 + fw],
                                     kpe_cb, [wb0, wb1], [ps0, ps1], n_free=T)
                  kve = ctx.sem("a_kve")
                  kx2 = sb(esk, nc, "kx2", [32, T], BF16)
                  nc.vector.tensor_copy(kx2[:, :], kpe[32:64, :])
                  nc.vector.tensor_tensor(kper[0:32, :], kpe[0:32, :], cos_sb[:, :], ALU.mult)
                  nc.vector.tensor_tensor(ktmp[:, :], kx2[:, :], sin_sb[:, :], ALU.mult)
                  nc.vector.tensor_sub(kper[0:32, :], kper[0:32, :], ktmp[:, :])
                  nc.vector.tensor_tensor(ktmp[:, :], kpe[0:32, :], sin_sb[:, :], ALU.mult)
                  nc.vector.tensor_tensor(kx2[:, :], kx2[:, :], cos_sb[:, :], ALU.mult)
                  nc.vector.tensor_add(ktmp[:, :], ktmp[:, :], kx2[:, :])
                  kve.inc(nc.vector.tensor_copy(kper[32:64, :], ktmp[:, :]))
                  ctx.wait(nc.gpsimd, kve)
                  stk.inc(nc.gpsimd.dma_start(out=kpe_st, in_=kper[:, :]), 16)
                  ctx.wait(nc.gpsimd, stk)
              ctx.bar()
            if "decode" in parts or "dg" in parts:
              with contextlib.ExitStack() as esd:
                  emit_decode(ctx, esd, "dg", idxg, O_PAD, D, lutb1, identb, gate_T,
                              pad_row0=O_PAD - 80)
            if "decode" in parts or "du" in parts:
              with contextlib.ExitStack() as esd:
                  emit_decode(ctx, esd, "du", idxu, O_PAD, D, lutb1, identb, up_T,
                              pad_row0=O_PAD - 80)
            if "decode" in parts or "dd" in parts:
              with contextlib.ExitStack() as esd:
                  emit_decode(ctx, esd, "dd", idxd, D, O_PAD, lutb1, identb, down_T,
                              pad_col0=O_PAD - 80)
            if "absum" in parts:
              with contextlib.ExitStack() as esa:
                  qw = [sb(esa, nc, "aqw%d" % i, [128, D], F32) for i in range(2)]
                  qs = sb(esa, nc, "aqs", [128, 8], F32)
                  qcols = sb(esa, nc, "aqc", [128, 24], F32)
                  ald = ctx.sem("ald")
                  ave = ctx.sem("ave")
                  amm = ctx.sem("amm")
                  ast = ctx.sem("ast")
                  aps = psum(esa, nc, "aps", [1, 2])
                  nc.vector.memset(qs[:, 2:3], 1.0)
                  i = 0
                  for wi, wsl in enumerate((w1s, w2s)):
                      rows, cols = wsl.shape
                      for rt in range(rows // 128):
                          if i >= 2:
                              ctx.wait(nc.sync, ave, i - 1)
                          ald.inc(nc.sync.dma_start(out=qw[i % 2][:, 0:cols],
                                                    in_=wsl[rt * 128:(rt + 1) * 128, :]), 16)
                          ctx.wait(nc.vector, ald)
                          ave.inc(nc.vector.reduce_sum(qcols[:, i:i + 1],
                                                       qw[i % 2][:, 0:cols], AX,
                                                       apply_absolute_value=True))
                          i += 1
                  ave.inc(nc.vector.nop())
                  ctx.wait(nc.vector, ave)
                  ctx.vflush(nc.vector.reduce_sum(qs[:, 0:1], qcols[:, 0:8], AX), "afl")
                  ctx.vflush(nc.vector.reduce_sum(qs[:, 1:2], qcols[:, 8:24], AX), "afl")
                  ave.inc(nc.vector.nop())
                  ctx.wait(nc.tensor, ave)
                  amm.inc(nc.tensor.matmul(aps[0:1, 0:2], qs[:, 2:3], qs[:, 0:2],
                                           start=True, stop=True))
                  ctx.wait(nc.vector, amm)
                  ctx.vflush(nc.vector.tensor_copy(qs[0:1, 4:6], aps[0:1, 0:2]), "afl")
                  ave.inc(nc.vector.nop())
                  ctx.wait(nc.gpsimd, ave)
                  ast.inc(nc.gpsimd.dma_start(out=absum, in_=qs[0:1, 4:6]), 16)
                  ctx.wait(nc.gpsimd, ast)
        ctx.bar()
    return nc


def inputs_A(state):
    full = state["full"]
    ident = np.eye(128, dtype=np.float32).astype(BF)
    w_uq = np.asarray(full["L0_attn_w_uq"], np.float32)
    w_uqp = np.zeros((Q_LORA, 4096), np.float32)
    for h in range(NH):
        blk = w_uq[:, h * QKD:(h + 1) * QKD]
        w_uqp[:, h * 256:h * 256 + 128] = blk[:, :NOPE]
        w_uqp[:, h * 256 + 128:h * 256 + 160] = blk[:, NOPE:NOPE + 32]
        w_uqp[:, h * 256 + 160:h * 256 + 192] = blk[:, NOPE + 32:NOPE + 64]
    gate_idx = np.zeros((NC * O_PAD, D), np.int32)
    up_idx = np.zeros((NC * O_PAD, D), np.int32)
    down_idx = np.zeros((D, NC * O_PAD), np.int32)
    gi = np.asarray(full["L0_ffn_gate_idx"], np.int32)
    ui = np.asarray(full["L0_ffn_up_idx"], np.int32)
    di = np.asarray(full["L0_ffn_down_idx"], np.int32)
    for r in range(NC):
        n = min(688, INTER - r * 688)
        gate_idx[r * O_PAD:r * O_PAD + n] = gi[r * 688:r * 688 + n]
        up_idx[r * O_PAD:r * O_PAD + n] = ui[r * 688:r * 688 + n]
        down_idx[:, r * O_PAD:r * O_PAD + n] = di[:, r * 688:r * 688 + n]
    perm = np.concatenate([np.arange(0, HD, 2), np.arange(1, HD, 2)])
    wq1 = np.asarray(full["L1_attn_wq"], np.float32).reshape(D, NH, HD)[:, :, perm].reshape(D, D)
    wk1 = np.asarray(full["L1_attn_wk"], np.float32).reshape(D, NH, HD)[:, :, perm].reshape(D, D)
    wv1 = np.asarray(full["L1_attn_wv"], np.float32).reshape(D, D)
    wo1 = np.asarray(full["L1_attn_wo"], np.float32).reshape(D, D)
    wo0 = np.asarray(full["L0_attn_w_o"], np.float32)
    inv_freq = 1.0 / (10000.0 ** (np.arange(0, ROPE, 2, dtype=np.float32) / ROPE))
    x = state["x"]
    maps = []
    for r in range(NC):
        pos = np.arange(r * T, (r + 1) * T, dtype=np.float32)
        fr = np.outer(inv_freq, pos)
        m = {
            "x_in": np.ascontiguousarray(x[r * T:(r + 1) * T]),
            "ident": ident,
            "wdq_f": np.asarray(full["L0_attn_w_dq"], np.float32),
            "wuq_f": w_uqp,
            "wdkv_f": np.asarray(full["L0_attn_w_dkv"], np.float32),
            "wkr_f": np.asarray(full["L0_attn_w_kr"], np.float32),
            "wuk_f": np.asarray(full["L0_attn_w_uk"], np.float32),
            "wuv_f": np.asarray(full["L0_attn_w_uv"], np.float32),
            "gq_v": np.ascontiguousarray(np.asarray(full["L0_attn_g_q"], np.float32).reshape(6, 128).T),
            "gkv_v": np.ascontiguousarray(np.asarray(full["L0_attn_g_kv"], np.float32).reshape(4, 128).T),
            "cos0": np.cos(fr).astype(BF),
            "sin0": np.sin(fr).astype(BF),
            "idxg": np.ascontiguousarray(gate_idx[r * O_PAD:(r + 1) * O_PAD]),
            "idxu": np.ascontiguousarray(up_idx[r * O_PAD:(r + 1) * O_PAD]),
            "idxd": np.ascontiguousarray(down_idx[:, r * O_PAD:(r + 1) * O_PAD]),
            "lut_in": np.asarray(full["L0_ffn_lut"], np.float32).reshape(1, 16),
            "w1s": np.ascontiguousarray(np.asarray(full["L1_ffn_w1"], np.float32)[r * 1024:(r + 1) * 1024]),
            "w2s": np.ascontiguousarray(np.asarray(full["L1_ffn_w2"], np.float32)[:, r * 1024:(r + 1) * 1024]),
            "wo0_s": np.ascontiguousarray(wo0[r * 256:(r + 1) * 256]),
            "wq1_s": np.ascontiguousarray(wq1[r * 256:(r + 1) * 256]),
            "wk1_s": np.ascontiguousarray(wk1[r * 256:(r + 1) * 256]),
            "wv1_s": np.ascontiguousarray(wv1[r * 256:(r + 1) * 256]),
            "wo1_s": np.ascontiguousarray(wo1[r * 256:(r + 1) * 256]),
        }
        maps.append(m)
    return maps


def run_A(state):
    nc = build_A()
    res = run_bass_kernel_spmd(nc, inputs_A(state), core_ids=list(range(NC)))
    return res.results



def emit_quantize(ctx, es, w1s, w2s, absums_in, w1q_st, w2q_st, ident_bf,
                  onesf):
    nc = ctx.nc
    ld = ctx.sem("quld")
    ve = ctx.sem("quve")
    mm = ctx.sem("qumm")
    st = ctx.sem("qust")
    tse = ctx.sem("quts")
    with contextlib.ExitStack() as es2:
        qw = sb(es2, nc, "quw", [128, D], F32)
        qs = sb(es2, nc, "qus", [128, 8], F32)
        arow = sb(es2, nc, "quar", [1, 16], F32)
        qbf = sb(es2, nc, "qubf", [128, D], BF16)
        qtr = sb(es2, nc, "qutr", [128, D], BF16)
        qps = psum(es2, nc, "qups", [128, 8])
        tpb = [psum(es2, nc, "qutp0", [128, 512], BF16),
               psum(es2, nc, "qutp1", [128, 512], BF16)]
        ld.inc(nc.sync.dma_start(out=arow[:, :], in_=absums_in), 16)
        ctx.wait(nc.vector, ld)
        ctx.vflush(nc.vector.reduce_sum(qs[0:1, 4:5], arow[0:1, 0:8], AX), "qufl")
        ctx.vflush(nc.vector.reduce_sum(qs[0:1, 5:6], arow[0:1, 8:16], AX), "qufl")
        ctx.vflush(nc.vector.tensor_scalar(qs[0:1, 4:6], qs[0:1, 4:6],
                                           1.0 / (DFF * D), 1e-8, ALU.mult, ALU.max), "qufl")
        ctx.vflush(nc.vector.reciprocal(qs[0:1, 4:6], qs[0:1, 4:6]), "qufl")
        ve.inc(nc.vector.nop())
        ctx.wait(nc.tensor, ve)
        mm.inc(nc.tensor.matmul(qps[:, 4:6], onesf[0:1, :], qs[0:1, 4:6],
                                start=True, stop=True))
        ctx.wait(nc.vector, mm)
        ctx.vflush(nc.vector.tensor_copy(qs[:, 4:6], qps[:, 4:6]), "qufl")
        ve.inc(nc.vector.nop())
        tb = 0
        for wi, (wsl, stage) in enumerate(((w1s, w1q_st), (w2s, w2q_st))):
            rows, cols = wsl.shape
            for rt in range(rows // 128):
                if not (wi == 0 and rt == 0):
                    ctx.wait(nc.sync, st)
                ld.inc(nc.sync.dma_start(out=qw[:, 0:cols],
                                         in_=wsl[rt * 128:(rt + 1) * 128, :]), 16)
                ctx.wait(nc.vector, ld)
                ctx.wait(nc.vector, st)
                nc.vector.tensor_scalar(qw[:, 0:cols], qw[:, 0:cols],
                                        qs[:, 4 + wi:5 + wi], None, ALU.mult)
                nc.vector.tensor_scalar(qw[:, 0:cols], qw[:, 0:cols], RBIG, RBIG,
                                        ALU.add, ALU.subtract)
                ve.inc(nc.vector.tensor_scalar(qbf[:, 0:cols], qw[:, 0:cols],
                                               1.0, -1.0, ALU.min, ALU.max))
                ctx.wait(nc.tensor, ve)
                for g in range(cols // 512):
                    tp = tpb[tb % 2]
                    if tb >= 2:
                        ctx.wait(nc.tensor, tse, tb - 1)
                    for c4 in range(4):
                        c = g * 4 + c4
                        inst = nc.tensor.transpose(tp[:, c4 * 128:(c4 + 1) * 128],
                                                   qbf[:, c * 128:(c + 1) * 128],
                                                   ident_bf[:, :])
                    mm.inc(inst)
                    ctx.wait(nc.vector, mm)
                    for c4 in range(4):
                        c = g * 4 + c4
                        inst = nc.vector.tensor_copy(qtr[:, c * 128:(c + 1) * 128],
                                                     tp[:, c4 * 128:(c4 + 1) * 128])
                    tse.inc(inst)
                    tb += 1
                ctx.wait(nc.gpsimd, tse)
                for c in range(cols // 128):
                    st.inc(nc.gpsimd.dma_start(
                        out=stage[c * 128:(c + 1) * 128, rt * 128:(rt + 1) * 128],
                        in_=qtr[:, c * 128:(c + 1) * 128]), 16)
        ctx.wait(nc.gpsimd, st)
    ctx.bar()


def emit_attn_head(ctx, es, hl, k_main, km_rows, k_aug, q_main, qm_rows, q_aug,
                   v_sb, ssq_kpe, masks, attn_st, onesb, onesf, sqs, rowA, rowB,
                   skip=()):
    nc = ctx.nc
    tg = "ah%d" % hl
    se = ctx.sem("se_" + tg)
    ve = ctx.sem("ve_" + tg)
    mm = ctx.sem("mm_" + tg)
    s_p = ctx.sem("sp_" + tg)
    s_pm = ctx.sem("spm_" + tg)
    s_pv = ctx.sem("spv_" + tg)
    s_nm = ctx.sem("snm_" + tg)
    stq = ctx.sem("stq_" + tg)

    _c = [0]
    if "norms" in skip:
        nc.vector.memset(q_aug[64:65, :], -10.0)

    def ssq_pass(main_ap, main_rows, aug_ap, out_row):
        _c[0] += 1
        with contextlib.ExitStack() as esn:
            nps = psum(esn, nc, "nps%d_" % _c[0] + tg, [1, 512])
            sq64 = sb(esn, nc, "sq64%d_" % _c[0] + tg, [64, 512], BF16)
            se.inc(nc.scalar.activation(sqs[0:main_rows, :], main_ap, AF.Square))
            for sl in range(16):
                if aug_ap is not None:
                    ctx.wait(nc.scalar, mm)
                    se.inc(nc.scalar.activation(
                        sq64[:, :], aug_ap[0:64, sl * 512:(sl + 1) * 512], AF.Square))
                ctx.wait(nc.tensor, se)
                inst = nc.tensor.matmul(nps[0:1, :], onesb[0:main_rows, 0:1],
                                        sqs[0:main_rows, sl * 512:(sl + 1) * 512],
                                        start=True, stop=(aug_ap is None))
                if aug_ap is not None:
                    inst = nc.tensor.matmul(nps[0:1, :], onesb[0:64, 0:1], sq64[:, :],
                                            start=False, stop=True)
                mm.inc(inst)
                ctx.wait(nc.vector, mm)
                ve.inc(nc.vector.tensor_copy(out_row[0:1, sl * 512:(sl + 1) * 512],
                                             nps[0:1, :]))
                ctx.wait(nc.tensor, ve)

    if "norms" not in skip:
        ssq_pass(k_main[:, :], km_rows, None if ssq_kpe is not None else k_aug, rowA)
        if ssq_kpe is not None:
            ctx.wait(nc.vector, ve)
            ctx.vflush(nc.vector.tensor_add(rowA[0:1, :], rowA[0:1, :],
                                            ssq_kpe[0:1, :]), "nfl_" + tg)
        ctx.vflush(nc.vector.reduce_max(rowB[0:1, 0:1], rowA[0:1, :], AX), "nfl_" + tg)
        ve.inc(nc.vector.nop())
        ctx.wait(nc.scalar, ve)
        se.inc(nc.scalar.activation(rowB[0:1, 0:1], rowB[0:1, 0:1], AF.Sqrt))
        ssq_pass(q_main[:, :], qm_rows, q_aug, rowA)
        ctx.wait(nc.scalar, ve)
        se.inc(nc.scalar.activation(rowA[0:1, :], rowA[0:1, :], AF.Sqrt))
        ctx.wait(nc.vector, se)
        ctx.vflush(nc.vector.tensor_scalar(rowA[0:1, :], rowA[0:1, :],
                                           rowB[0:1, 0:1], -1.0, ALU.mult, ALU.mult),
                   "nfl_" + tg)
        ve.inc(nc.vector.tensor_copy(q_aug[64:65, :], rowA[0:1, :]))
        ctx.wait(nc.tensor, ve)
    ctx.bar()

    if "flash" in skip:
        return
    NS = 3
    with contextlib.ExitStack() as esf:
        scp = [psum(esf, nc, "scp%d_" % i + tg, [128, 512]) for i in range(NS)]
        pvp = psum(esf, nc, "pvp_" + tg, [128, 512])
        smp = psum(esf, nc, "smp_" + tg, [1, 512])
        bcp = psum(esf, nc, "bcp_" + tg, [128, 512])
        psb = [sb(esf, nc, "psb%d_" % i + tg, [128, 512], BF16) for i in range(NS)]
        srow = sb(esf, nc, "srow_" + tg, [1, 512], F32)
        rcb = sb(esf, nc, "rcb_" + tg, [128, 512], BF16)
        att = sb(esf, nc, "att_" + tg, [128, 512], BF16)
        for j in range(16):
            nk = 4 * j + 4
            pend = []
            for k in range(nk + 1):
                if k < nk:
                    diag = k >= 4 * j
                    if k >= NS:
                        ctx.wait(nc.tensor, s_p, s_p.n - (NS - 1))
                    nc.tensor.matmul(scp[k % NS][:, :],
                                     k_main[:, k * 128:(k + 1) * 128],
                                     q_main[:, j * 512:(j + 1) * 512],
                                     start=True, stop=False)
                    mm.inc(nc.tensor.matmul(scp[k % NS][:, :],
                                            k_aug[:, k * 128:(k + 1) * 128],
                                            q_aug[:, j * 512:(j + 1) * 512],
                                            start=False, stop=True))
                    nc.scalar.wait_ge(mm.h, mm.n)
                    if k >= NS:
                        nc.scalar.wait_ge(s_pv.h, s_pv.n)
                    s_p.inc(nc.scalar.activation(psb[k % NS][:, :],
                                                 scp[k % NS][:, :], AF.Exp))
                    if diag:
                        c4 = k - 4 * j
                        nc.vector.wait_ge(s_p.h, s_p.n)
                        s_pm.inc(nc.vector.tensor_tensor(
                            psb[k % NS][:, :], psb[k % NS][:, :],
                            masks[:, c4 * 512:(c4 + 1) * 512], ALU.mult))
                    pend.append((k, diag, s_p.n, s_pm.n))
                while pend and (len(pend) >= NS or k == nk):
                    pk, pdiag, spn, spmn = pend.pop(0)
                    if pdiag:
                        ctx.wait(nc.tensor, s_pm, spmn)
                    else:
                        ctx.wait(nc.tensor, s_p, spn)
                    if pk == 0:
                        ctx.wait(nc.tensor, s_nm)
                    nc.tensor.matmul(pvp[:, :], v_sb[:, pk * 128:(pk + 1) * 128],
                                     psb[pk % NS][:, :], start=(pk == 0),
                                     stop=(pk == nk - 1), skip_group_check=True)
                    s_pv.inc(nc.tensor.matmul(smp[0:1, :], onesb[:, 0:1],
                                              psb[pk % NS][:, :], start=(pk == 0),
                                              stop=(pk == nk - 1),
                                              skip_group_check=True))
            ctx.wait(nc.vector, s_pv)
            ctx.vflush(nc.vector.reciprocal(srow[0:1, :], smp[0:1, :]),
                       "rfl_" + tg)
            ve.inc(nc.vector.nop())
            ctx.wait(nc.tensor, ve)
            mm.inc(nc.tensor.matmul(bcp[:, :], onesf[0:1, :], srow[0:1, :],
                                    start=True, stop=True))
            ctx.wait(nc.vector, mm)
            nc.vector.tensor_copy(rcb[:, :], bcp[:, :])
            if j > 0:
                ctx.wait(nc.vector, stq)
            s_nm.inc(nc.vector.tensor_tensor(att[:, :], pvp[:, :], rcb[:, :],
                                             ALU.mult))
            s_nm.inc(nc.vector.nop())
            ctx.wait(nc.gpsimd, s_nm)
            stq.inc(nc.gpsimd.dma_start(
                out=attn_st[hl * 128:(hl + 1) * 128, j * 512:(j + 1) * 512],
                in_=att[:, :]), 16)
        ctx.wait(nc.gpsimd, stq)
    ctx.bar()


def build_B(skip=()):
    nc = bass.Bass("TRN2", target_bir_lowering=False)
    ctx = Ctx(nc)
    kn_in = nc.dram_tensor("kn_in", [2 * NOPE, S], BF16, kind="ExternalInput").ap()
    vt_in = nc.dram_tensor("vt_in", [S, 2 * VD], BF16, kind="ExternalInput").ap()
    kpe_in = nc.dram_tensor("kpe_in", [ROPE, S], BF16, kind="ExternalInput").ap()
    qn_in = nc.dram_tensor("qn_in", [2 * NOPE, S], BF16, kind="ExternalInput").ap()
    qr_in = nc.dram_tensor("qr_in", [2 * ROPE, S], BF16, kind="ExternalInput").ap()
    masks_in = nc.dram_tensor("masks_in", [128, 4 * 512], BF16, kind="ExternalInput").ap()
    absums_in = nc.dram_tensor("absums_in", [1, 16], F32, kind="ExternalInput").ap()
    w1s = nc.dram_tensor("w1s", [DFF // NC, D], F32, kind="ExternalInput").ap()
    w2s = nc.dram_tensor("w2s", [D, DFF // NC], F32, kind="ExternalInput").ap()
    ident_in = nc.dram_tensor("ident", [128, 128], BF16, kind="ExternalInput").ap()

    attn_st = nc.dram_tensor("attn_st", [2 * VD, S], BF16, kind="ExternalOutput").ap()
    w1q_st = nc.dram_tensor("w1q_st", [D, T], BF16, kind="ExternalOutput").ap()
    w2q_st = nc.dram_tensor("w2q_st", [T, D], BF16, kind="ExternalOutput").ap()

    with contextlib.ExitStack() as es:
        es.enter_context(nc.Block())
        identb = sb(es, nc, "identb", [128, 128], BF16)
        onesb = sb(es, nc, "onesb", [128, 1], BF16)
        onesf = sb(es, nc, "onesf", [1, 128], F32)
        masks = sb(es, nc, "masks", [128, 4 * 512], BF16)
        kr_sb = sb(es, nc, "kr_sb", [65, S], BF16)
        ssq_kpe = sb(es, nc, "ssq_kpe", [1, S], F32)
        ld0 = ctx.sem("ld0")
        ld0.inc(nc.sync.dma_start(out=identb[:, :], in_=ident_in), 16)
        ld0.inc(nc.sync.dma_start(out=masks[:, :], in_=masks_in), 16)
        ld0.inc(nc.sync.dma_start(out=kr_sb[0:64, :], in_=kpe_in), 16)
        nc.vector.memset(onesb[:, :], 1.0)
        nc.vector.memset(onesf[:, :], 1.0)
        ctx.wait(nc.vector, ld0)
        nc.vector.memset(kr_sb[64:65, :], 1.0)
        ctx.wait(nc.tensor, ld0)
        ctx.bar()

        emit_quantize(ctx, es, w1s, w2s, absums_in, w1q_st, w2q_st, identb, onesf)

        with contextlib.ExitStack() as esn:
            sqk = sb(esn, nc, "sqk", [64, S], BF16)
            nps = psum(esn, nc, "npsk", [1, 512])
            kse = ctx.sem("kse")
            kmm = ctx.sem("kmm")
            kve = ctx.sem("kve")
            kse.inc(nc.scalar.activation(sqk[:, :], kr_sb[0:64, :], AF.Square))
            ctx.wait(nc.tensor, kse)
            for sl in range(16):
                kmm.inc(nc.tensor.matmul(nps[0:1, :], onesb[0:64, 0:1],
                                         sqk[:, sl * 512:(sl + 1) * 512],
                                         start=True, stop=True))
                ctx.wait(nc.vector, kmm)
                kve.inc(nc.vector.tensor_copy(ssq_kpe[0:1, sl * 512:(sl + 1) * 512],
                                              nps[0:1, :]))
                ctx.wait(nc.tensor, kve)
        ctx.bar()

        for hl in range(2):
            with contextlib.ExitStack() as eh:
                kn_h = sb(eh, nc, "knh%d" % hl, [128, S], BF16)
                v_sb = sb(eh, nc, "vsb%d" % hl, [128, 64 * 128], BF16)
                qn_h = sb(eh, nc, "qnh%d" % hl, [128, S], BF16)
                qraug = sb(eh, nc, "qra%d" % hl, [65, S], BF16)
                sqs = sb(eh, nc, "sqs%d" % hl, [128, S], BF16)
                rowA = sb(eh, nc, "rwa%d" % hl, [1, S], F32)
                rowB = sb(eh, nc, "rwb%d" % hl, [1, 16], F32)
                hld = ctx.sem("hld%d" % hl)
                hld.inc(nc.sync.dma_start(out=kn_h[:, :],
                                          in_=kn_in[hl * 128:(hl + 1) * 128, :]), 16)
                hld.inc(nc.sync.dma_start(out=qn_h[:, :],
                                          in_=qn_in[hl * 128:(hl + 1) * 128, :]), 16)
                hld.inc(nc.sync.dma_start(out=qraug[0:64, :],
                                          in_=qr_in[hl * 64:(hl + 1) * 64, :]), 16)
                vsrc = vt_in[:, hl * 128:(hl + 1) * 128].rearrange(
                    "(b p) d -> p b d", p=128)
                vdst = v_sb[:, :].rearrange("p (b d) -> p b d", d=128)
                hld.inc(nc.sync.dma_start(out=vdst, in_=vsrc), 16)
                ctx.wait(nc.tensor, hld)
                ctx.wait(nc.scalar, hld)
                ctx.wait(nc.vector, hld)
                ctx.bar()
                emit_attn_head(ctx, eh, hl, kn_h, 128, kr_sb, qn_h, 128, qraug,
                               v_sb, ssq_kpe, masks, attn_st, onesb, onesf, sqs,
                               rowA, rowB, skip=skip)
    return nc


def inputs_B(state, layer=0):
    full = state["full"]
    ident = np.eye(128, dtype=np.float32).astype(BF)
    masks = np.zeros((128, 4, 512), np.float32)
    tk = np.arange(128)[:, None]
    tq = np.arange(512)[None, :]
    for ci, c in enumerate((0, 128, 256, 384)):
        masks[:, ci, :] = (tq >= tk + c)
    masks = masks.reshape(128, 2048).astype(BF)
    kn = state["kn_full"]
    vt = state["vt_full"]
    qn = state["qn_full"]
    qr = state["qr_full"]
    kpe = state.get("kpe_full")
    maps = []
    for r in range(NC):
        m = {
            "ident": ident, "masks_in": masks,
            "kn_in": np.ascontiguousarray(kn[r * 256:(r + 1) * 256]),
            "vt_in": np.ascontiguousarray(vt[:, r * 256:(r + 1) * 256]),
            "qn_in": np.ascontiguousarray(qn[r * 256:(r + 1) * 256]),
            "qr_in": np.ascontiguousarray(qr[r * 128:(r + 1) * 128]),
            "kpe_in": np.ascontiguousarray(kpe),
            "absums_in": state["absums"],
            "w1s": np.ascontiguousarray(np.asarray(full["L1_ffn_w1"], np.float32)[r * 1024:(r + 1) * 1024]),
            "w2s": np.ascontiguousarray(np.asarray(full["L1_ffn_w2"], np.float32)[:, r * 1024:(r + 1) * 1024]),
        }
        maps.append(m)
    return maps


def run_B(state):
    nc = build_B()
    res = run_bass_kernel_spmd(nc, inputs_B(state), core_ids=list(range(NC)))
    return res.results



def build_D():
    nc = bass.Bass("TRN2", target_bir_lowering=False)
    ctx = Ctx(nc)
    k1_in = nc.dram_tensor("k1_in", [128, S], BF16, kind="ExternalInput").ap()
    k2_in = nc.dram_tensor("k2_in", [128, S], BF16, kind="ExternalInput").ap()
    q1_in = nc.dram_tensor("q1_in", [128, S], BF16, kind="ExternalInput").ap()
    q2_in = nc.dram_tensor("q2_in", [128, S], BF16, kind="ExternalInput").ap()
    vt_in = nc.dram_tensor("vt_in", [S, 2 * VD], BF16, kind="ExternalInput").ap()
    masks_in = nc.dram_tensor("masks_in", [128, 4 * 512], BF16, kind="ExternalInput").ap()
    ident_in = nc.dram_tensor("ident", [128, 128], BF16, kind="ExternalInput").ap()
    attn_st = nc.dram_tensor("attn_st", [2 * VD, S], BF16, kind="ExternalOutput").ap()

    with contextlib.ExitStack() as es:
        es.enter_context(nc.Block())
        identb = sb(es, nc, "identb", [128, 128], BF16)
        onesb = sb(es, nc, "onesb", [128, 1], BF16)
        onesf = sb(es, nc, "onesf", [1, 128], F32)
        masks = sb(es, nc, "masks", [128, 4 * 512], BF16)
        ld0 = ctx.sem("ld0")
        ld0.inc(nc.sync.dma_start(out=identb[:, :], in_=ident_in), 16)
        ld0.inc(nc.sync.dma_start(out=masks[:, :], in_=masks_in), 16)
        nc.vector.memset(onesb[:, :], 1.0)
        nc.vector.memset(onesf[:, :], 1.0)
        ctx.wait(nc.tensor, ld0)
        ctx.wait(nc.vector, ld0)
        ctx.bar()
        for hl in range(2):
            with contextlib.ExitStack() as eh:
                k2_h = sb(eh, nc, "k2h%d" % hl, [64, S], BF16)
                k1s = sb(eh, nc, "k1s%d" % hl, [65, S], BF16)
                q2_h = sb(eh, nc, "q2h%d" % hl, [64, S], BF16)
                q1aug = sb(eh, nc, "q1a%d" % hl, [65, S], BF16)
                v_sb = sb(eh, nc, "vsb%d" % hl, [128, 64 * 128], BF16)
                sqs = sb(eh, nc, "sqs%d" % hl, [128, S], BF16)
                rowA = sb(eh, nc, "rwa%d" % hl, [1, S], F32)
                rowB = sb(eh, nc, "rwb%d" % hl, [1, 16], F32)
                hld = ctx.sem("hld%d" % hl)
                hld.inc(nc.sync.dma_start(out=k2_h[:, :],
                                          in_=k2_in[hl * 64:(hl + 1) * 64, :]), 16)
                hld.inc(nc.sync.dma_start(out=k1s[0:64, :],
                                          in_=k1_in[hl * 64:(hl + 1) * 64, :]), 16)
                hld.inc(nc.sync.dma_start(out=q2_h[:, :],
                                          in_=q2_in[hl * 64:(hl + 1) * 64, :]), 16)
                hld.inc(nc.sync.dma_start(out=q1aug[0:64, :],
                                          in_=q1_in[hl * 64:(hl + 1) * 64, :]), 16)
                vsrc = vt_in[:, hl * 128:(hl + 1) * 128].rearrange(
                    "(b p) d -> p b d", p=128)
                vdst = v_sb[:, :].rearrange("p (b d) -> p b d", d=128)
                hld.inc(nc.sync.dma_start(out=vdst, in_=vsrc), 16)
                ctx.wait(nc.vector, hld)
                nc.vector.memset(k1s[64:65, :], 1.0)
                ctx.wait(nc.tensor, hld)
                ctx.wait(nc.scalar, hld)
                ctx.bar()
                emit_attn_head(ctx, eh, hl, k2_h, 64, k1s, q2_h, 64, q1aug,
                               v_sb, None, masks, attn_st, onesb, onesf, sqs,
                               rowA, rowB)
    return nc


def inputs_D(state):
    ident = np.eye(128, dtype=np.float32).astype(BF)
    masks = np.zeros((128, 4, 512), np.float32)
    tk = np.arange(128)[:, None]
    tq = np.arange(512)[None, :]
    for ci, c in enumerate((0, 128, 256, 384)):
        masks[:, ci, :] = (tq >= tk + c)
    masks = masks.reshape(128, 2048).astype(BF)
    maps = []
    for r in range(NC):
        maps.append({
            "ident": ident, "masks_in": masks,
            "k1_in": np.ascontiguousarray(state["k1_full"][r * 128:(r + 1) * 128]),
            "k2_in": np.ascontiguousarray(state["k2_full"][r * 128:(r + 1) * 128]),
            "q1_in": np.ascontiguousarray(state["q1_full"][r * 128:(r + 1) * 128]),
            "q2_in": np.ascontiguousarray(state["q2_full"][r * 128:(r + 1) * 128]),
            "vt_in": np.ascontiguousarray(state["vt1_full"][:, r * 256:(r + 1) * 256]),
        })
    return maps


def run_D(state):
    nc = build_D()
    res = run_bass_kernel_spmd(nc, inputs_D(state), core_ids=list(range(NC)))
    return res.results



def build_C():
    nc = bass.Bass("TRN2", target_bir_lowering=False)
    ctx = Ctx(nc)
    x_in = nc.dram_tensor("x_in", [T, D], F32, kind="ExternalInput").ap()
    x_out = nc.dram_tensor("x_out", [T, D], F32, kind="ExternalOutput").ap()
    ident_in = nc.dram_tensor("ident", [128, 128], BF16, kind="ExternalInput").ap()
    attn_in = nc.dram_tensor("attn_in", [D, T], BF16, kind="ExternalInput").ap()
    wo_in = nc.dram_tensor("wo_in", [D, D], BF16, kind="ExternalInput").ap()
    gate_in = nc.dram_tensor("gate_in", [D, NC * O_PAD], BF16, kind="ExternalInput").ap()
    up_in = nc.dram_tensor("up_in", [D, NC * O_PAD], BF16, kind="ExternalInput").ap()
    down_in = nc.dram_tensor("down_in", [NC * O_PAD, D], BF16, kind="ExternalInput").ap()
    wq_in = nc.dram_tensor("wq_in", [D, D], BF16, kind="ExternalInput").ap()
    wk_in = nc.dram_tensor("wk_in", [D, D], BF16, kind="ExternalInput").ap()
    wv_in = nc.dram_tensor("wv_in", [D, D], BF16, kind="ExternalInput").ap()
    cos1q = nc.dram_tensor("cos1q", [64, T], BF16, kind="ExternalInput").ap()
    sin1q = nc.dram_tensor("sin1q", [64, T], BF16, kind="ExternalInput").ap()
    cos1k = nc.dram_tensor("cos1k", [64, T], BF16, kind="ExternalInput").ap()
    sin1k = nc.dram_tensor("sin1k", [64, T], BF16, kind="ExternalInput").ap()

    q1_st = nc.dram_tensor("q1_st", [NH * 64, T], BF16, kind="ExternalOutput").ap()
    q2_st = nc.dram_tensor("q2_st", [NH * 64, T], BF16, kind="ExternalOutput").ap()
    k1_st = nc.dram_tensor("k1_st", [NH * 64, T], BF16, kind="ExternalOutput").ap()
    k2_st = nc.dram_tensor("k2_st", [NH * 64, T], BF16, kind="ExternalOutput").ap()
    vt1_st = nc.dram_tensor("vt1_st", [T, D], BF16, kind="ExternalOutput").ap()

    with contextlib.ExitStack() as es:
        x_tiles = [sb(es, nc, "x%d" % t, [128, D], F32) for t in range(8)]
        identb = sb(es, nc, "identb", [128, 128], BF16)
        onesb = sb(es, nc, "onesb", [128, 1], BF16)
        onesf = sb(es, nc, "onesf", [1, 128], F32)
        es.enter_context(nc.Block())
        ld0 = ctx.sem("ld0")
        for t in range(8):
            ld0.inc(nc.sync.dma_start(out=x_tiles[t][:, :],
                                      in_=x_in[t * 128:(t + 1) * 128, :]), 16)
        ld0.inc(nc.sync.dma_start(out=identb[:, :], in_=ident_in), 16)
        nc.vector.memset(onesb[:, :], 1.0)
        nc.vector.memset(onesf[:, :], 1.0)

        with contextlib.ExitStack() as es2:
            atT = sb(es2, nc, "atT", [128, 16 * T], BF16)
            woS = sb(es2, nc, "woT", [128, 16 * D], BF16)
            for c in range(16):
                ld0.inc(nc.sync.dma_start(out=atT[:, c * T:(c + 1) * T],
                                          in_=attn_in[c * 128:(c + 1) * 128, :]), 16)
                ld0.inc(nc.sync.dma_start(out=woS[:, c * D:(c + 1) * D],
                                          in_=wo_in[c * 128:(c + 1) * 128, :]), 16)
            ctx.wait(nc.tensor, ld0)
            ctx.wait(nc.vector, ld0)
            ctx.bar()
            emit_token_proj(ctx, "c_ao",
                            lambda kc, tg: atT[:, kc * T + tg * 128:kc * T + (tg + 1) * 128],
                            16, [woS[:, c * D:(c + 1) * D] for c in range(16)],
                            x_tiles)

        with contextlib.ExitStack() as es3:
            xlt = sb(es3, nc, "xlt", [128, 16 * T], BF16)
            emit_ln(ctx, x_tiles, xlt, identb, "c_ln1")
            NOC = NC * O_PAD // 128
            for th in range(2):
                tq0 = th * 512
                with contextlib.ExitStack() as es4:
                    gsil = sb(es4, nc, "gsil%d" % th, [128, NOC * 512], BF16)
                    hT = sb(es4, nc, "hT%d" % th, [128, NOC * 512], BF16)
                    with contextlib.ExitStack() as es5:
                        wb0 = sb(es5, nc, "cwb0_%d" % th, [128, 16 * 128], BF16)
                        wb1 = sb(es5, nc, "cwb1_%d" % th, [128, 16 * 128], BF16)
                        ps0 = psum(es5, nc, "cps0_%d" % th, [128, 512])
                        ps1 = psum(es5, nc, "cps1_%d" % th, [128, 512])

                        def g_cb(m, f0, fw, ps, mmn, _th=th):
                            nc.scalar.wait_ge(ctx.sem("mmd_c_g%d" % _th).h, mmn)
                            return nc.scalar.activation(
                                gsil[:, m * 512:m * 512 + fw], ps[:, 0:fw], AF.Silu)

                        emit_stream_mm(ctx, "c_g%d" % th, plain_pieces(gate_in, D),
                                       D, NC * O_PAD,
                                       lambda kc, kk, f0, fw, _tq0=tq0: xlt[:, kc * T + _tq0 + f0:kc * T + _tq0 + f0 + fw],
                                       g_cb, [wb0, wb1], [ps0, ps1], n_free=512)

                        def u_cb(m, f0, fw, ps, mmn, _th=th):
                            nc.vector.wait_ge(ctx.sem("mmd_c_u%d" % _th).h, mmn)
                            return nc.vector.tensor_tensor(
                                hT[:, m * 512:m * 512 + fw], ps[:, 0:fw],
                                gsil[:, m * 512:m * 512 + fw], ALU.mult)

                        emit_stream_mm(ctx, "c_u%d" % th, plain_pieces(up_in, D),
                                       D, NC * O_PAD,
                                       lambda kc, kk, f0, fw, _tq0=tq0: xlt[:, kc * T + _tq0 + f0:kc * T + _tq0 + f0 + fw],
                                       u_cb, [wb0, wb1], [ps0, ps1], n_free=512)

                    tag2 = "c_d%d" % th
                    dld = ctx.sem("dld_" + tag2)
                    dfr = ctx.sem("dfr_" + tag2)
                    dmm = ctx.sem("dmm_" + tag2)
                    dve = ctx.sem("dve_" + tag2)
                    with contextlib.ExitStack() as es6:
                        db = [sb(es6, nc, "cdb%d_%d" % (i, th), [128, D], BF16)
                              for i in range(3)]
                        ops = [psum(es6, nc, "cop%d_%d" % (i, th), [128, 512])
                               for i in range(8)]

                        def d_load(oc):
                            dld.inc(nc.sync.dma_start(
                                out=db[oc % 2][:, :],
                                in_=down_in[oc * 128:(oc + 1) * 128, :]), 16)

                        for dch in range(2):
                            ctx.wait(nc.sync, dfr)
                            for ff in range(2):
                                d_load(ff)
                            nld = 2
                            for oc in range(NOC):
                                ctx.wait(nc.tensor, dld, 16 * (oc + 1) + 16 * NOC * dch)
                                for tg in range(4):
                                    for d2 in range(2):
                                        inst = nc.tensor.matmul(
                                            ops[tg * 2 + d2][:, :],
                                            hT[:, oc * 512 + tg * 128:oc * 512 + (tg + 1) * 128],
                                            db[oc % 2][:, dch * 1024 + d2 * 512:
                                                       dch * 1024 + (d2 + 1) * 512],
                                            start=(oc == 0), stop=(oc == NOC - 1))
                                dfr.inc(inst)
                                if nld < NOC:
                                    ctx.wait(nc.sync, dfr, dfr.n)
                                    d_load(nld)
                                    nld += 1
                            ctx.wait(nc.vector, dfr)
                            for tg in range(4):
                                xt = x_tiles[th * 4 + tg]
                                for d2 in range(2):
                                    inst = nc.vector.scalar_tensor_tensor(
                                        xt[:, dch * 1024 + d2 * 512:dch * 1024 + (d2 + 1) * 512],
                                        ops[tg * 2 + d2][:, :], 1.0,
                                        xt[:, dch * 1024 + d2 * 512:dch * 1024 + (d2 + 1) * 512],
                                        ALU.mult, ALU.add)
                            dve.inc(inst)
                            ctx.wait(nc.tensor, dve)
                        ctx.bar()

        with contextlib.ExitStack() as es7:
            xlt2 = sb(es7, nc, "xlt2", [128, 16 * T], BF16)
            emit_ln(ctx, x_tiles, xlt2, identb, "c_ln2")
            rld = ctx.sem("rld")
            ctabs = {}
            for nm2, apz in (("cq", cos1q), ("sq", sin1q), ("ck", cos1k), ("sk", sin1k)):
                ctabs[nm2] = sb(es7, nc, "tab_" + nm2, [64, T], BF16)
                rld.inc(nc.sync.dma_start(out=ctabs[nm2][:, :], in_=apz), 16)
            ctx.wait(nc.vector, rld)

            def qk_pass(tag3, w_ap, cosT, sinT, st1, st2):
                with contextlib.ExitStack() as es8:
                    qb = sb(es8, nc, "qb_" + tag3, [128, 16 * T], BF16)
                    with contextlib.ExitStack() as es9:
                        wb0 = sb(es9, nc, "wb0_" + tag3, [128, 16 * 128], BF16)
                        wb1 = sb(es9, nc, "wb1_" + tag3, [128, 16 * 128], BF16)
                        ps0 = psum(es9, nc, "ps0_" + tag3, [128, 512])
                        ps1 = psum(es9, nc, "ps1_" + tag3, [128, 512])

                        def cb(m, f0, fw, ps, mmn):
                            nc.vector.wait_ge(ctx.sem("mmd_" + tag3).h, mmn)
                            return nc.vector.tensor_copy(
                                qb[:, m * T + f0:m * T + f0 + fw], ps[:, 0:fw])

                        emit_stream_mm(ctx, tag3, plain_pieces(w_ap, D), D, D,
                                       lambda kc, kk, f0, fw: xlt2[:, kc * T + f0:kc * T + f0 + fw],
                                       cb, [wb0, wb1], [ps0, ps1], n_free=T)
                    o1t = sb(es8, nc, "o1_" + tag3, [64, 4 * T], BF16)
                    x2c = sb(es8, nc, "x2_" + tag3, [64, 4 * T], BF16)
                    rtm = sb(es8, nc, "rt_" + tag3, [64, 4 * T], BF16)
                    rve = ctx.sem("rv_" + tag3)
                    rst = ctx.sem("rs_" + tag3)
                    for g4 in range(4):
                        if g4 > 0:
                            ctx.wait(nc.vector, rst)
                        x1 = qb[0:64, :].rearrange("p (c t) -> p c t", t=T)[:, g4 * 4:(g4 + 1) * 4, :]
                        x2s = qb[64:128, :].rearrange("p (c t) -> p c t", t=T)[:, g4 * 4:(g4 + 1) * 4, :]
                        x2v = x2c[:, :].rearrange("p (c t) -> p c t", t=T)
                        o1 = o1t[:, :].rearrange("p (c t) -> p c t", t=T)
                        tm = rtm[:, :].rearrange("p (c t) -> p c t", t=T)
                        cb4 = cosT[:, :].rearrange("p t -> p () t").broadcast_to([64, 4, T])
                        sb4 = sinT[:, :].rearrange("p t -> p () t").broadcast_to([64, 4, T])
                        nc.vector.tensor_copy(x2v, x2s)
                        nc.vector.tensor_tensor(o1, x1, cb4, ALU.mult)
                        nc.vector.tensor_tensor(tm, x2v, sb4, ALU.mult)
                        nc.vector.tensor_sub(o1, o1, tm)
                        nc.vector.tensor_tensor(tm, x1, sb4, ALU.mult)
                        nc.vector.tensor_tensor(x2v, x2v, cb4, ALU.mult)
                        rve.inc(nc.vector.tensor_add(tm, tm, x2v))
                        ctx.wait(nc.gpsimd, rve)
                        for hh in range(4):
                            h = g4 * 4 + hh
                            rst.inc(nc.gpsimd.dma_start(
                                out=st1[h * 64:(h + 1) * 64, :],
                                in_=o1t[:, hh * T:(hh + 1) * T]), 16)
                            rst.inc(nc.gpsimd.dma_start(
                                out=st2[h * 64:(h + 1) * 64, :],
                                in_=rtm[:, hh * T:(hh + 1) * T]), 16)
                    ctx.wait(nc.gpsimd, rst)
                ctx.bar()

            qk_pass("c_q", wq_in, ctabs["cq"], ctabs["sq"], q1_st, q2_st)
            qk_pass("c_k", wk_in, ctabs["ck"], ctabs["sk"], k1_st, k2_st)

            with contextlib.ExitStack() as es8:
                vb = sb(es8, nc, "vb", [128, 16 * T], BF16)
                with contextlib.ExitStack() as es9:
                    wb0 = sb(es9, nc, "vwb0", [128, 16 * 128], BF16)
                    wb1 = sb(es9, nc, "vwb1", [128, 16 * 128], BF16)
                    ps0 = psum(es9, nc, "vps0", [128, 512])
                    ps1 = psum(es9, nc, "vps1", [128, 512])

                    def v_cb(m, f0, fw, ps, mmn):
                        nc.vector.wait_ge(ctx.sem("mmd_c_v").h, mmn)
                        return nc.vector.tensor_copy(vb[:, m * T + f0:m * T + f0 + fw],
                                                     ps[:, 0:fw])

                    emit_stream_mm(ctx, "c_v", plain_pieces(wv_in, D), D, D,
                                   lambda kc, kk, f0, fw: xlt2[:, kc * T + f0:kc * T + f0 + fw],
                                   v_cb, [wb0, wb1], [ps0, ps1], n_free=T)
                with contextlib.ExitStack() as es9:
                    tpb = [psum(es9, nc, "vtp%d" % i, [128, 512], BF16) for i in range(2)]
                    vrow = sb(es9, nc, "vrow", [128, 512], BF16)
                    tmm = ctx.sem("vtmm")
                    tve = ctx.sem("vtve")
                    tst = ctx.sem("vtst")
                    tb = 0
                    for tt in range(8):
                        for dcg in range(4):
                            tp = tpb[tb % 2]
                            if tb >= 2:
                                ctx.wait(nc.tensor, tve, tb - 1)
                            for c4 in range(4):
                                c = dcg * 4 + c4
                                inst = nc.tensor.transpose(
                                    tp[:, c4 * 128:(c4 + 1) * 128],
                                    vb[:, c * T + tt * 128:c * T + (tt + 1) * 128],
                                    identb[:, :])
                            tmm.inc(inst)
                            ctx.wait(nc.vector, tmm)
                            if tb >= 1:
                                ctx.wait(nc.vector, tst, 16 * tb)
                            tve.inc(nc.vector.tensor_copy(vrow[:, :], tp[:, :]))
                            ctx.wait(nc.gpsimd, tve)
                            tst.inc(nc.gpsimd.dma_start(
                                out=vt1_st[tt * 128:(tt + 1) * 128,
                                           dcg * 512:(dcg + 1) * 512],
                                in_=vrow[:, :]), 16)
                            tb += 1
                    ctx.wait(nc.gpsimd, tst)
                ctx.bar()

        ctx.bar()
        sto = ctx.sem("sto")
        for t in range(8):
            sto.inc(nc.gpsimd.dma_start(out=x_out[t * 128:(t + 1) * 128, :],
                                        in_=x_tiles[t][:, :]), 16)
        ctx.wait(nc.gpsimd, sto)
    return nc


def inputs_C(state):
    full = state["full"]
    ident = np.eye(128, dtype=np.float32).astype(BF)
    half = HD // 2
    fraction = 2.0 * np.arange(half, dtype=np.float32) / HD
    timescale = 10000.0 ** fraction
    maps = []
    for r in range(NC):
        pos = np.arange(r * T, (r + 1) * T, dtype=np.float32)
        sinu = pos[None, :] / timescale[:, None]
        cosv, sinv = np.cos(sinu), np.sin(sinu)
        m = {
            "x_in": np.ascontiguousarray(state["x"][r * T:(r + 1) * T]),
            "ident": ident,
            "attn_in": np.ascontiguousarray(state["attn0T"][:, r * T:(r + 1) * T]),
            "wo_in": state["wo0_bf"],
            "gate_in": state["gate_all"],
            "up_in": state["up_all"],
            "down_in": state["down_all"],
            "wq_in": state["wq1_bf"],
            "wk_in": state["wk1_bf"],
            "wv_in": state["wv1_bf"],
            "cos1q": (cosv * L1_SCALE).astype(BF),
            "sin1q": (sinv * L1_SCALE).astype(BF),
            "cos1k": cosv.astype(BF),
            "sin1k": sinv.astype(BF),
        }
        maps.append(m)
    return maps


def run_C(state):
    nc = build_C()
    res = run_bass_kernel_spmd(nc, inputs_C(state), core_ids=list(range(NC)))
    return res.results



def build_E(parts=("ao", "ffn"), dbg=()):
    nc = bass.Bass("TRN2", target_bir_lowering=False)
    ctx = Ctx(nc)
    dbg_out = {name: nc.dram_tensor(name, shape, dt, kind="ExternalOutput").ap()
               for name, shape, dt in dbg}
    x_in = nc.dram_tensor("x_in", [T, D], F32, kind="ExternalInput").ap()
    x_out = nc.dram_tensor("x_out", [T, D], F32, kind="ExternalOutput").ap()
    ident_in = nc.dram_tensor("ident", [128, 128], BF16, kind="ExternalInput").ap()
    attn_in = nc.dram_tensor("attn_in", [D, T], BF16, kind="ExternalInput").ap()
    wo_in = nc.dram_tensor("wo_in", [D, D], BF16, kind="ExternalInput").ap()
    w1q_in = nc.dram_tensor("w1q_in", [NC * D, T], BF16, kind="ExternalInput").ap()
    w2q_in = nc.dram_tensor("w2q_in", [NC * T, D], BF16, kind="ExternalInput").ap()
    b1v = nc.dram_tensor("b1v", [128, 64], F32, kind="ExternalInput").ap()
    b2r = nc.dram_tensor("b2r", [1, D], BF16, kind="ExternalInput").ap()

    with contextlib.ExitStack() as es:
        x_tiles = [sb(es, nc, "x%d" % t, [128, D], F32) for t in range(8)]
        identb = sb(es, nc, "identb", [128, 128], BF16)
        es.enter_context(nc.Block())
        ld0 = ctx.sem("ld0")
        for t in range(8):
            ld0.inc(nc.sync.dma_start(out=x_tiles[t][:, :],
                                      in_=x_in[t * 128:(t + 1) * 128, :]), 16)
        ld0.inc(nc.sync.dma_start(out=identb[:, :], in_=ident_in), 16)

        with contextlib.ExitStack() as es2:
            atT = sb(es2, nc, "atT", [128, 16 * T], BF16)
            woS = sb(es2, nc, "woT", [128, 16 * D], BF16)
            for c in range(16):
                ld0.inc(nc.sync.dma_start(out=atT[:, c * T:(c + 1) * T],
                                          in_=attn_in[c * 128:(c + 1) * 128, :]), 16)
                ld0.inc(nc.sync.dma_start(out=woS[:, c * D:(c + 1) * D],
                                          in_=wo_in[c * 128:(c + 1) * 128, :]), 16)
            ctx.wait(nc.tensor, ld0)
            ctx.bar()
            if "ao" in parts:
                emit_token_proj(ctx, "e_ao",
                                lambda kc, tg: atT[:, kc * T + tg * 128:kc * T + (tg + 1) * 128],
                                16, [woS[:, c * D:(c + 1) * D] for c in range(16)],
                                x_tiles)
        if "dbg_x_ao" in dbg_out:
            dsem = ctx.sem("dbg1")
            for t in range(8):
                dsem.inc(nc.gpsimd.dma_start(out=dbg_out["dbg_x_ao"][t * 128:(t + 1) * 128, :],
                                             in_=x_tiles[t][:, :]), 16)
            ctx.wait(nc.gpsimd, dsem)
            ctx.bar()

        ld = ctx.sem("f1ld")
        with contextlib.ExitStack() as es3:
            xlt = sb(es3, nc, "xlt", [128, 16 * T], BF16)
            b1sb = sb(es3, nc, "b1sb", [128, 64], F32)
            b2sb = sb(es3, nc, "b2sb", [1, D], BF16)
            ones1 = sb(es3, nc, "ones1", [1, 128], BF16)
            ld.inc(nc.sync.dma_start(out=b1sb[:, :], in_=b1v), 16)
            ld.inc(nc.gpsimd.dma_start(out=b2sb[:, :], in_=b2r), 16)
            nc.vector.memset(ones1[:, :], 1.0)
            emit_ln(ctx, x_tiles, xlt, identb, "e_ln")
            ctx.wait(nc.scalar, ld)
            ctx.wait(nc.tensor, ld)
            if "dbg_xlt" in dbg_out:
                dsem2 = ctx.sem("dbg2")
                dsem2.inc(nc.gpsimd.dma_start(out=dbg_out["dbg_xlt"], in_=xlt[:, :]), 16)
                ctx.wait(nc.gpsimd, dsem2)
                ctx.bar()
            if "ffn" not in parts:
                ctx.sem("skip")
            hT_dump = "dbg_hT" in dbg_out

            for th in range(2 if "ffn" in parts else 0):
                tq0 = th * 512
                with contextlib.ExitStack() as es4:
                    hT = sb(es4, nc, "hT%d" % th, [128, 64 * 512], BF16)
                    tag = "eh%d" % th
                    with contextlib.ExitStack() as es5:
                        w1b0 = sb(es5, nc, "w1b0_%d" % th, [128, 16 * 128], BF16)
                        w1b1 = sb(es5, nc, "w1b1_%d" % th, [128, 16 * 128], BF16)
                        fps0 = psum(es5, nc, "fps0_%d" % th, [128, 512])
                        fps1 = psum(es5, nc, "fps1_%d" % th, [128, 512])
                        def w1_pieces(m):
                            r, fl = divmod(m, 8)
                            return [(w1q_in[r * D + kc * 128:r * D + (kc + 1) * 128,
                                            fl * 128:(fl + 1) * 128], 0, 128, kc * 128, 128)
                                    for kc in range(16)]

                        def rhs_fn(kc, kk, f0, fw, _tq0=tq0):
                            return xlt[:, kc * T + _tq0 + f0:kc * T + _tq0 + f0 + fw]

                        def out_cb(m, f0, fw, ps, mmn, _tag=tag):
                            nc.scalar.wait_ge(ctx.sem("mmd_" + _tag).h, mmn)
                            return nc.scalar.activation(hT[:, m * 512:m * 512 + fw],
                                                        ps[:, 0:fw], AF.Gelu,
                                                        bias=b1sb[:, m:m + 1])

                        emit_stream_mm(ctx, tag, w1_pieces, D, DFF, rhs_fn, out_cb,
                                       [w1b0, w1b1], [fps0, fps1], n_free=512)
                    if th == 0 and hT_dump:
                        dsem3 = ctx.sem("dbg3")
                        dsem3.inc(nc.gpsimd.dma_start(out=dbg_out["dbg_hT"], in_=hT[:, :]), 16)
                        ctx.wait(nc.gpsimd, dsem3)
                        ctx.bar()

                    tag2 = "eo%d" % th
                    w2ld = ctx.sem("w2ld_" + tag2)
                    w2fr = ctx.sem("w2fr_" + tag2)
                    mm2 = ctx.sem("mm2_" + tag2)
                    ve2 = ctx.sem("ve2_" + tag2)
                    with contextlib.ExitStack() as es6:
                        w2bufs4 = [sb(es6, nc, "w2b%d_%d" % (i, th), [128, D], BF16)
                                   for i in range(4)]
                        ops = [psum(es6, nc, "ops%d_%d" % (i, th), [128, 512]) for i in range(8)]

                        def w2_load(f):
                            r, fl = divmod(f, 8)
                            w2ld.inc(nc.sync.dma_start(
                                out=w2bufs4[f % 2][:, :],
                                in_=w2q_in[r * T + fl * 128:r * T + (fl + 1) * 128, :]), 16)

                        for dch in range(2):
                            ctx.wait(nc.sync, w2fr)
                            w2_load(0)
                            w2_load(1)
                            nld = 2
                            for f in range(64):
                                ctx.wait(nc.tensor, w2ld, 16 * (f + 1) + 16 * 64 * dch)
                                for tg in range(4):
                                    for d2 in range(2):
                                        inst = nc.tensor.matmul(
                                            ops[tg * 2 + d2][:, :],
                                            hT[:, f * 512 + tg * 128:f * 512 + (tg + 1) * 128],
                                            w2bufs4[f % 2][:, dch * 1024 + d2 * 512:
                                                           dch * 1024 + (d2 + 1) * 512],
                                            start=(f == 0), stop=False)
                                w2fr.inc(inst)
                                if nld < 64:
                                    ctx.wait(nc.sync, w2fr, w2fr.n)
                                    w2_load(nld)
                                    nld += 1
                            for tg in range(4):
                                for d2 in range(2):
                                    inst = nc.tensor.matmul(
                                        ops[tg * 2 + d2][:, :], ones1[:, :],
                                        b2sb[:, dch * 1024 + d2 * 512:dch * 1024 + (d2 + 1) * 512],
                                        start=False, stop=True)
                            mm2.inc(inst)
                            ctx.wait(nc.vector, mm2)
                            for tg in range(4):
                                xt = x_tiles[th * 4 + tg]
                                for d2 in range(2):
                                    inst = nc.vector.scalar_tensor_tensor(
                                        xt[:, dch * 1024 + d2 * 512:dch * 1024 + (d2 + 1) * 512],
                                        ops[tg * 2 + d2][:, :], 1.0,
                                        xt[:, dch * 1024 + d2 * 512:dch * 1024 + (d2 + 1) * 512],
                                        ALU.mult, ALU.add)
                            ve2.inc(inst)
                            ctx.wait(nc.tensor, ve2)
                        ctx.bar()

        ctx.bar()
        sto = ctx.sem("sto")
        for t in range(8):
            sto.inc(nc.gpsimd.dma_start(out=x_out[t * 128:(t + 1) * 128, :],
                                        in_=x_tiles[t][:, :]), 16)
        ctx.wait(nc.gpsimd, sto)
    return nc


def inputs_E(state):
    ident = np.eye(128, dtype=np.float32).astype(BF)
    full = state["full"]
    maps = []
    for r in range(NC):
        maps.append({
            "x_in": np.ascontiguousarray(state["x"][r * T:(r + 1) * T]),
            "ident": ident,
            "attn_in": np.ascontiguousarray(state["attn1T"][:, r * T:(r + 1) * T]),
            "wo_in": state["wo1_bf"],
            "w1q_in": state["w1q"],
            "w2q_in": state["w2q"],
            "b1v": np.ascontiguousarray(np.asarray(full["L1_ffn_b1"], np.float32).reshape(64, 128).T),
            "b2r": np.asarray(full["L1_ffn_b2"], np.float32).reshape(1, D).astype(BF),
        })
    return maps


def run_E(state):
    nc = build_E()
    res = run_bass_kernel_spmd(nc, inputs_E(state), core_ids=list(range(NC)))
    return np.concatenate([res.results[r]["x_out"] for r in range(NC)], axis=0)


def kernel(**inputs):
    full = inputs
    x = np.ascontiguousarray(np.asarray(full["L0_attn_x"], np.float32).reshape(S, D))
    state = {"full": full, "x": x}

    resA = run_A(state)
    cat = np.concatenate
    state["qn_full"] = cat([resA[r]["qn_st"] for r in range(NC)], axis=1)
    state["qr_full"] = cat([resA[r]["qr_st"] for r in range(NC)], axis=1)
    state["kn_full"] = cat([resA[r]["kn_st"] for r in range(NC)], axis=1)
    state["vt_full"] = cat([resA[r]["vt_st"] for r in range(NC)], axis=0)
    state["kpe_full"] = cat([resA[r]["kpe_st"] for r in range(NC)], axis=1)
    state["absums"] = np.asarray(
        [[resA[r]["absum"][0, 0] for r in range(NC)] +
         [resA[r]["absum"][0, 1] for r in range(NC)]], np.float32)
    state["gate_all"] = np.ascontiguousarray(cat([resA[r]["gate_T"] for r in range(NC)], axis=1))
    state["up_all"] = np.ascontiguousarray(cat([resA[r]["up_T"] for r in range(NC)], axis=1))
    state["down_all"] = np.ascontiguousarray(cat([resA[r]["down_T"] for r in range(NC)], axis=0))
    for nm in ("wo0", "wq1", "wk1", "wv1", "wo1"):
        state[nm + "_bf"] = np.ascontiguousarray(cat([resA[r][nm + "_bf"] for r in range(NC)], axis=0))

    resB = run_B(state)
    state["attn0T"] = np.ascontiguousarray(cat([resB[r]["attn_st"] for r in range(NC)], axis=0))
    state["w1q"] = np.ascontiguousarray(cat([resB[r]["w1q_st"] for r in range(NC)], axis=0))
    state["w2q"] = np.ascontiguousarray(cat([resB[r]["w2q_st"] for r in range(NC)], axis=0))

    resC = run_C(state)
    state["x"] = np.ascontiguousarray(cat([resC[r]["x_out"] for r in range(NC)], axis=0))
    state["q1_full"] = cat([resC[r]["q1_st"] for r in range(NC)], axis=1)
    state["q2_full"] = cat([resC[r]["q2_st"] for r in range(NC)], axis=1)
    state["k1_full"] = cat([resC[r]["k1_st"] for r in range(NC)], axis=1)
    state["k2_full"] = cat([resC[r]["k2_st"] for r in range(NC)], axis=1)
    state["vt1_full"] = cat([resC[r]["vt1_st"] for r in range(NC)], axis=0)

    resD = run_D(state)
    state["attn1T"] = np.ascontiguousarray(cat([resD[r]["attn_st"] for r in range(NC)], axis=0))
    state["wo1_bf"] = state["wo1_bf"]

    x = run_E(state)
    return x.reshape(1, S, D).astype(np.float32)


# version 08
# speedup vs baseline: 1.0093x; 1.0093x over previous
import contextlib
import numpy as np
import ml_dtypes
import concourse.bass as bass
import concourse.mybir as mybir
from concourse.bass_utils import run_bass_kernel_spmd

F32 = mybir.dt.float32
BF16 = mybir.dt.bfloat16
I32 = mybir.dt.int32
AF = mybir.ActivationFunctionType
ALU = mybir.AluOpType
AX = mybir.AxisListType.X
BF = ml_dtypes.bfloat16

NC = 8
S = 8192
D = 2048
T = S // NC
NH = 16

NOPE, ROPE, VD = 128, 64, 128
QKD = NOPE + ROPE
Q_LORA, KV_LORA = 768, 512
L0_SCALE = QKD ** -0.5
RMS_EPS = 1e-6
LN_EPS = 1e-5
INTER = 5504
O_PAD = 768
HD = 128
L1_SCALE = HD ** -0.5
DFF = 8192
RBIG = 12582912.0


class Sem:
    __slots__ = ("h", "n")

    def __init__(self, nc, name):
        self.h = nc.alloc_semaphore(name)
        self.n = 0

    def inc(self, inst, k=1):
        inst.then_inc(self.h, k)
        self.n += k
        return self.n


class Ctx:
    def __init__(self, nc):
        self.nc = nc
        self.sems = {}

    def sem(self, name):
        if name not in self.sems:
            self.sems[name] = Sem(self.nc, name)
        return self.sems[name]

    def wait(self, eng, sem, val=None):
        eng.wait_ge(sem.h, sem.n if val is None else val)

    def vflush(self, inst, tag="vfl"):
        s = self.sem(tag)
        s.inc(inst)
        self.nc.vector.wait_ge(s.h, s.n)

    def bar(self):
        self.nc.all_engine_barrier()


def cdiv(a, b):
    return (a + b - 1) // b


def sb(es, nc, name, shape, dtype):
    return es.enter_context(nc.sbuf_tensor(name, shape, dtype))


def psum(es, nc, name, shape, dtype=F32):
    return es.enter_context(nc.psum_tensor(name, shape, dtype))




def emit_ln(ctx, x_tiles, xlt, ident_bf, tag, dbg_stats=None):
    nc = ctx.nc
    se = ctx.sem("lnse_" + tag)
    ve = ctx.sem("lnve_" + tag)
    mm = ctx.sem("lnmm_" + tag)
    tev = ctx.sem("lntev_" + tag)
    with (
        nc.sbuf_tensor("lnstats_" + tag, [128, 32], F32) as stats,
        nc.sbuf_tensor("lnsq_" + tag, [128, D], F32) as sq,
        nc.sbuf_tensor("lnxln_" + tag, [128, D], BF16) as xln,
        nc.psum_tensor("lntp0_" + tag, [128, 512], BF16) as tp0,
        nc.psum_tensor("lntp1_" + tag, [128, 512], BF16) as tp1,
    ):
        tps = [tp0, tp1]
        ve.inc(nc.vector.memset(stats[:, 8:16], 0.0))
        ctx.wait(nc.scalar, ve)
        for t in range(8):
            nc.vector.reduce_sum(stats[:, t:t + 1], x_tiles[t][:, :], AX)
            if t > 0:
                nc.scalar.wait_ge(se.h, se.n)
            se.inc(nc.scalar.activation(sq[:, :], x_tiles[t][:, :], AF.Square,
                                        accum_out=stats[:, 8 + t:9 + t]))
        ctx.wait(nc.vector, se)
        fl = "lnfl_" + tag
        ctx.vflush(nc.vector.tensor_scalar(stats[:, 16:24], stats[:, 0:8], 1.0 / D, None, ALU.mult), fl)
        ctx.vflush(nc.vector.tensor_tensor(stats[:, 24:32], stats[:, 16:24], stats[:, 16:24], ALU.mult), fl)
        ctx.vflush(nc.vector.tensor_scalar(stats[:, 8:16], stats[:, 8:16], 1.0 / D, None, ALU.mult), fl)
        ctx.vflush(nc.vector.tensor_sub(stats[:, 24:32], stats[:, 8:16], stats[:, 24:32]), fl)
        ve.inc(nc.vector.tensor_scalar(stats[:, 24:32], stats[:, 24:32], LN_EPS, None, ALU.add))
        ctx.wait(nc.scalar, ve)
        se.inc(nc.scalar.activation(stats[:, 24:32], stats[:, 24:32], AF.Sqrt))
        ctx.wait(nc.vector, se)
        ctx.vflush(nc.vector.reciprocal(stats[:, 24:32], stats[:, 24:32]), fl)
        ve.inc(nc.vector.nop())
        if dbg_stats is not None:
            ctx.wait(nc.gpsimd, ve)
            dsem = ctx.sem("lndbg_" + tag)
            dsem.inc(nc.gpsimd.dma_start(out=dbg_stats, in_=stats[:, :]), 16)
            ctx.wait(nc.gpsimd, dsem)

        ng = 0
        for t in range(8):
            if t > 0:
                ctx.wait(nc.vector, mm, ng)
            ve.inc(nc.vector.tensor_scalar(
                xln[:, :], x_tiles[t][:, :], stats[:, 16 + t:17 + t],
                stats[:, 24 + t:25 + t], ALU.subtract, ALU.mult))
            ctx.wait(nc.tensor, ve)
            for g in range(4):
                b = ng % 2
                if ng >= 2:
                    ctx.wait(nc.tensor, tev, ng - 1)
                for c4 in range(4):
                    c = g * 4 + c4
                    inst = nc.tensor.transpose(tps[b][:, c4 * 128:(c4 + 1) * 128],
                                               xln[:, c * 128:(c + 1) * 128],
                                               ident_bf[:, :])
                mm.inc(inst)
                ctx.wait(nc.vector, mm)
                for c4 in range(4):
                    c = g * 4 + c4
                    inst = nc.vector.tensor_copy(
                        xlt[:, c * T + t * 128:c * T + (t + 1) * 128],
                        tps[b][:, c4 * 128:(c4 + 1) * 128])
                tev.inc(inst)
                ng += 1
    ctx.bar()



def emit_stream_mm(ctx, tag, slab_pieces, K, M, rhs_fn, out_cb, wbufs, psum,
                   n_free, free_tile=512):
    nc = ctx.nc
    nk = cdiv(K, 128)
    nm = cdiv(M, 128)
    nf = cdiv(n_free, free_tile)
    wld = ctx.sem("wld_" + tag)
    wfree = ctx.sem("wfree_" + tag)
    mmd = ctx.sem("mmd_" + tag)
    oev = ctx.sem("oev_" + tag)
    for m in range(min(2, nm)):
        for ap, row0, nrows, col0, mw in slab_pieces(m):
            wld.inc(nc.sync.dma_start(out=wbufs[m % 2][row0:row0 + nrows, col0:col0 + mw],
                                      in_=ap), 16)
    npc = len(slab_pieces(0))
    npc_ev = None
    g = 0
    for m in range(nm):
        buf = wbufs[m % 2]
        ctx.wait(nc.tensor, wld, 16 * npc * (m + 1))
        for f in range(nf):
            f0 = f * free_tile
            fw = min(free_tile, n_free - f0)
            ps = psum[g % len(psum)]
            if g >= len(psum):
                ctx.wait(nc.tensor, oev, npc_ev * (g - len(psum) + 1))
            mw = min(128, M - m * 128)
            for kc in range(nk):
                kk = min(128, K - kc * 128)
                inst = nc.tensor.matmul(ps[0:mw, 0:fw], buf[0:kk, kc * 128:kc * 128 + mw],
                                        rhs_fn(kc, kk, f0, fw),
                                        start=(kc == 0), stop=(kc == nk - 1))
            mmd.inc(inst)
            insts = out_cb(m, f0, fw, ps, mmd.n)
            if not isinstance(insts, (list, tuple)):
                insts = [insts]
            for ii in insts:
                oev.inc(ii)
            if npc_ev is None:
                npc_ev = len(insts)
            g += 1
        if m + 2 < nm:
            ctx.wait(nc.sync, mmd, nf * (m + 1))
            for ap, row0, nrows, col0, mw in slab_pieces(m + 2):
                wld.inc(nc.sync.dma_start(out=wbufs[m % 2][row0:row0 + nrows, col0:col0 + mw],
                                          in_=ap), 16)
    ctx.bar()



def emit_token_proj(ctx, tag, lhs_fn, nkc, w_tiles, x_tiles, extra_rhs=None):
    nc = ctx.nc
    mm = ctx.sem("tpmm_" + tag)
    ve = ctx.sem("tpve_" + tag)
    with (
        nc.psum_tensor("tpps0_" + tag, [128, 512], F32) as ps0,
        nc.psum_tensor("tpps1_" + tag, [128, 512], F32) as ps1,
    ):
        pss = [ps0, ps1]
        g = 0
        for tg in range(8):
            for dc in range(4):
                ps = pss[g % 2]
                if g >= 2:
                    ctx.wait(nc.tensor, ve, g - 1)
                for kc in range(nkc):
                    inst = nc.tensor.matmul(ps[:, :], lhs_fn(kc, tg),
                                            w_tiles[kc][:, dc * 512:(dc + 1) * 512],
                                            start=(kc == 0),
                                            stop=(kc == nkc - 1 and extra_rhs is None))
                if extra_rhs is not None:
                    ones_row, brow_fn = extra_rhs
                    inst = nc.tensor.matmul(ps[:, :], ones_row,
                                            brow_fn(dc * 512, 512),
                                            start=False, stop=True)
                mm.inc(inst)
                ctx.wait(nc.vector, mm)
                ve.inc(nc.vector.scalar_tensor_tensor(
                    x_tiles[tg][:, dc * 512:(dc + 1) * 512], ps[:, :], 1.0,
                    x_tiles[tg][:, dc * 512:(dc + 1) * 512], ALU.mult, ALU.add))
                g += 1
    ctx.bar()



def emit_rms(ctx, es_unused, tag, src_big, nk, nreal, g_ap, scale, out_big, ones_bf,
             ones_f32r):
    nc = ctx.nc
    se = ctx.sem("rmse_" + tag)
    ve = ctx.sem("rmve_" + tag)
    mm = ctx.sem("rmmm_" + tag)
    with contextlib.ExitStack() as es2:
        sqb = sb(es2, nc, "rmsq_" + tag, [128, nk * T], BF16)
        row = sb(es2, nc, "rmrow_" + tag, [1, T], F32)
        rstdB = sb(es2, nc, "rmbc_" + tag, [128, T], BF16)
        tmp = sb(es2, nc, "rmtmp_" + tag, [128, T], BF16)
        ps = psum(es2, nc, "rmps_" + tag, [1, 512])
        ps2 = psum(es2, nc, "rmps2_" + tag, [128, 512])
        se.inc(nc.scalar.activation(sqb[:, :], src_big[:, :], AF.Square))
        ctx.wait(nc.tensor, se)
        for half in range(2):
            for kc in range(nk):
                inst = nc.tensor.matmul(ps[0:1, :], ones_bf[:, 0:1],
                                        sqb[:, kc * T + half * 512:kc * T + half * 512 + 512],
                                        start=(kc == 0), stop=(kc == nk - 1))
            mm.inc(inst)
            ctx.wait(nc.vector, mm)
            ve.inc(nc.vector.tensor_copy(row[0:1, half * 512:half * 512 + 512], ps[0:1, :]))
            ctx.wait(nc.tensor, ve)
        fl = "rmfl_" + tag
        ctx.vflush(nc.vector.tensor_scalar(row[0:1, :], row[0:1, :], 1.0 / nreal,
                                           RMS_EPS, ALU.mult, ALU.add), fl)
        ve.inc(nc.vector.nop())
        ctx.wait(nc.scalar, ve)
        se.inc(nc.scalar.activation(row[0:1, :], row[0:1, :], AF.Sqrt))
        ctx.wait(nc.vector, se)
        ctx.vflush(nc.vector.reciprocal(row[0:1, :], row[0:1, :]), fl)
        ve.inc(nc.vector.nop())
        ctx.wait(nc.tensor, ve)
        for half in range(2):
            mm.inc(nc.tensor.matmul(ps2[:, :], ones_f32r[0:1, :],
                                    row[0:1, half * 512:half * 512 + 512],
                                    start=True, stop=True))
            ctx.wait(nc.vector, mm)
            ve.inc(nc.vector.tensor_copy(rstdB[:, half * 512:half * 512 + 512], ps2[:, :]))
            ctx.wait(nc.tensor, ve)
        for kc in range(nk):
            nc.vector.tensor_scalar(tmp[:, :], src_big[:, kc * T:(kc + 1) * T],
                                    g_ap[:, kc:kc + 1], float(scale), ALU.mult, ALU.mult)
            ve.inc(nc.vector.tensor_tensor(out_big[:, kc * T:(kc + 1) * T],
                                           tmp[:, :], rstdB[:, :], ALU.mult))
    ctx.bar()



def emit_decode(ctx, es, tag, idx_in, rows, cols, lutb1, ident_bf, stage_out,
                pad_row0=None, pad_col0=None, dec_out=None):
    nc = ctx.nc
    ld = ctx.sem("dcld_" + tag)
    ve = ctx.sem("dcve_" + tag)
    mm = ctx.sem("dcmm_" + tag)
    tse = ctx.sem("dcts_" + tag)
    st = ctx.sem("dcst_" + tag)
    idxt = sb(es, nc, "dcidx_" + tag, [128, cols], I32)
    idxb = sb(es, nc, "dcib_" + tag, [128, cols], BF16)
    acc = sb(es, nc, "dcacc_" + tag, [128, cols], BF16)
    tj = sb(es, nc, "dctj_" + tag, [128, cols], BF16)
    qtr = sb(es, nc, "dctr_" + tag, [128, cols], BF16)
    with contextlib.ExitStack() as es2:
        tpb = [psum(es2, nc, "dctp0_" + tag, [128, 512], BF16),
               psum(es2, nc, "dctp1_" + tag, [128, 512], BF16)]
        tb = 0
        for rt in range(rows // 128):
            if rt > 0:
                ctx.wait(nc.sync, st)
            ld.inc(nc.sync.dma_start(out=idxt[:, :],
                                     in_=idx_in[rt * 128:(rt + 1) * 128, :]), 16)
            ctx.wait(nc.vector, ld)
            ctx.wait(nc.vector, st)
            nc.vector.tensor_copy(idxb[:, :], idxt[:, :])
            for j in range(16):
                if j == 0:
                    inst = nc.vector.tensor_scalar(acc[:, :], idxb[:, :], 0.0,
                                                   lutb1[:, 0:1], ALU.is_equal, ALU.mult)
                else:
                    nc.vector.tensor_scalar(tj[:, :], idxb[:, :], float(j),
                                            lutb1[:, j:j + 1], ALU.is_equal, ALU.mult)
                    inst = nc.vector.tensor_add(acc[:, :], acc[:, :], tj[:, :])
            ve.inc(inst)
            if pad_col0 is not None:
                ve.inc(nc.vector.memset(acc[:, pad_col0:cols], 0.0))
            if dec_out is not None:
                ctx.wait(nc.gpsimd, ve)
                st.inc(nc.gpsimd.dma_start(out=dec_out[rt * 128:(rt + 1) * 128, :],
                                           in_=acc[:, :]), 16)
            ctx.wait(nc.tensor, ve)
            ncch = cols // 128
            for g in range(cdiv(ncch, 4)):
                n4 = min(4, ncch - g * 4)
                tp = tpb[tb % 2]
                if tb >= 2:
                    ctx.wait(nc.tensor, tse, tb - 1)
                for c4 in range(n4):
                    c = g * 4 + c4
                    inst = nc.tensor.transpose(tp[:, c4 * 128:(c4 + 1) * 128],
                                               acc[:, c * 128:(c + 1) * 128],
                                               ident_bf[:, :])
                mm.inc(inst)
                ctx.wait(nc.vector, mm)
                for c4 in range(n4):
                    c = g * 4 + c4
                    inst = nc.vector.tensor_copy(qtr[:, c * 128:(c + 1) * 128],
                                                 tp[:, c4 * 128:(c4 + 1) * 128])
                tse.inc(inst)
                tb += 1
            if pad_row0 is not None and rt == rows // 128 - 1:
                j0 = pad_row0 - rt * 128
                for c in range(cols // 128):
                    inst = nc.vector.memset(qtr[:, c * 128 + j0:(c + 1) * 128], 0.0)
                ve.inc(inst)
                ctx.wait(nc.gpsimd, ve)
            ctx.wait(nc.gpsimd, tse)
            for c in range(cols // 128):
                st.inc(nc.gpsimd.dma_start(
                    out=stage_out[c * 128:(c + 1) * 128, rt * 128:(rt + 1) * 128],
                    in_=qtr[:, c * 128:(c + 1) * 128]), 16)
    ctx.wait(nc.gpsimd, st)
    ctx.bar()



def plain_pieces(w_ap, K):
    nk = cdiv(K, 128)
    M = w_ap.shape[1]
    def pieces(m):
        mw = min(128, M - m * 128)
        return [(w_ap[kc * 128:min((kc + 1) * 128, K), m * 128:m * 128 + mw],
                 0, min(128, K - kc * 128), kc * 128, mw) for kc in range(nk)]
    return pieces


def build_A(parts=("qkv", "decode", "absum")):
    nc = bass.Bass("TRN2", target_bir_lowering=False)
    ctx = Ctx(nc)
    x_in = nc.dram_tensor("x_in", [T, D], F32, kind="ExternalInput").ap()
    ident_in = nc.dram_tensor("ident", [128, 128], BF16, kind="ExternalInput").ap()
    wdq_f = nc.dram_tensor("wdq_f", [D, Q_LORA], F32, kind="ExternalInput").ap()
    wuq_f = nc.dram_tensor("wuq_f", [Q_LORA, 4096], F32, kind="ExternalInput").ap()
    wdkv_f = nc.dram_tensor("wdkv_f", [D, KV_LORA], F32, kind="ExternalInput").ap()
    wkr_f = nc.dram_tensor("wkr_f", [D, ROPE], F32, kind="ExternalInput").ap()
    wuk_f = nc.dram_tensor("wuk_f", [KV_LORA, D], F32, kind="ExternalInput").ap()
    wuv_f = nc.dram_tensor("wuv_f", [KV_LORA, D], F32, kind="ExternalInput").ap()
    gq_v = nc.dram_tensor("gq_v", [128, 6], F32, kind="ExternalInput").ap()
    gkv_v = nc.dram_tensor("gkv_v", [128, 4], F32, kind="ExternalInput").ap()
    cos0 = nc.dram_tensor("cos0", [32, T], BF16, kind="ExternalInput").ap()
    sin0 = nc.dram_tensor("sin0", [32, T], BF16, kind="ExternalInput").ap()
    idxg = nc.dram_tensor("idxg", [O_PAD, D], I32, kind="ExternalInput").ap()
    idxu = nc.dram_tensor("idxu", [O_PAD, D], I32, kind="ExternalInput").ap()
    idxd = nc.dram_tensor("idxd", [D, O_PAD], I32, kind="ExternalInput").ap()
    lut_in = nc.dram_tensor("lut_in", [1, 16], F32, kind="ExternalInput").ap()
    w1s = nc.dram_tensor("w1s", [DFF // NC, D], F32, kind="ExternalInput").ap()
    w2s = nc.dram_tensor("w2s", [D, DFF // NC], F32, kind="ExternalInput").ap()
    cast_slices = {}
    for nm in ("wo0", "wq1", "wk1", "wv1", "wo1"):
        cast_slices[nm] = nc.dram_tensor(nm + "_s", [D // NC, D], F32, kind="ExternalInput").ap()

    qn_st = nc.dram_tensor("qn_st", [NH * NOPE, T], BF16, kind="ExternalOutput").ap()
    qr_st = nc.dram_tensor("qr_st", [NH * ROPE, T], BF16, kind="ExternalOutput").ap()
    kn_st = nc.dram_tensor("kn_st", [NH * NOPE, T], BF16, kind="ExternalOutput").ap()
    vt_st = nc.dram_tensor("vt_st", [T, D], BF16, kind="ExternalOutput").ap()
    kpe_st = nc.dram_tensor("kpe_st", [ROPE, T], BF16, kind="ExternalOutput").ap()
    gate_T = nc.dram_tensor("gate_T", [D, O_PAD], BF16, kind="ExternalOutput").ap()
    up_T = nc.dram_tensor("up_T", [D, O_PAD], BF16, kind="ExternalOutput").ap()
    down_T = nc.dram_tensor("down_T", [O_PAD, D], BF16, kind="ExternalOutput").ap()
    absum = nc.dram_tensor("absum", [1, 2], F32, kind="ExternalOutput").ap()
    cast_outs = {}
    for nm in ("wo0", "wq1", "wk1", "wv1", "wo1"):
        cast_outs[nm] = nc.dram_tensor(nm + "_bf", [D // NC, D], BF16, kind="ExternalOutput").ap()

    wdq_b = nc.dram_tensor("wdq_b", [D, Q_LORA], BF16)
    wuq_b = nc.dram_tensor("wuq_b", [Q_LORA, 4096], BF16)
    wdkv_b = nc.dram_tensor("wdkv_b", [D, KV_LORA], BF16)
    wkr_b = nc.dram_tensor("wkr_b", [D, ROPE], BF16)
    wuk_b = nc.dram_tensor("wuk_b", [KV_LORA, D], BF16)
    wuv_b = nc.dram_tensor("wuv_b", [KV_LORA, D], BF16)

    with contextlib.ExitStack() as es:
        es.enter_context(nc.Block())
        cld = ctx.sem("cld")
        for dst, src_ap in ((wdq_b, wdq_f), (wuq_b, wuq_f), (wdkv_b, wdkv_f),
                            (wkr_b, wkr_f), (wuk_b, wuk_f), (wuv_b, wuv_f)):
            cld.inc(nc.gpsimd.dma_start(out=dst.ap(), in_=src_ap), 16)
        for nm in ("wo0", "wq1", "wk1", "wv1", "wo1"):
            cld.inc(nc.gpsimd.dma_start(out=cast_outs[nm], in_=cast_slices[nm]), 16)

        with contextlib.ExitStack() as es1:
            xlt = sb(es, nc, "xlt", [128, 16 * T], BF16)
            identb = sb(es, nc, "identb", [128, 128], BF16)
            onesb = sb(es, nc, "onesb", [128, 1], BF16)
            onesf = sb(es, nc, "onesf", [1, 128], F32)
            gq_sb = sb(es, nc, "gq_sb", [128, 6], F32)
            gkv_sb = sb(es, nc, "gkv_sb", [128, 4], F32)
            lutb1 = sb(es, nc, "lutb1", [128, 16], F32)
            cos_sb = sb(es, nc, "cos_sb", [32, T], BF16)
            sin_sb = sb(es, nc, "sin_sb", [32, T], BF16)
            ld0 = ctx.sem("ld0")
            with contextlib.ExitStack() as esx:
                x_tiles = [sb(esx, nc, "x%d" % t, [128, D], F32) for t in range(8)]
                for t in range(8):
                    ld0.inc(nc.sync.dma_start(out=x_tiles[t][:, :],
                                              in_=x_in[t * 128:(t + 1) * 128, :]), 16)
                ld0.inc(nc.sync.dma_start(out=identb[:, :], in_=ident_in), 16)
                ld0.inc(nc.sync.dma_start(out=gq_sb[:, :], in_=gq_v), 16)
                ld0.inc(nc.sync.dma_start(out=gkv_sb[:, :], in_=gkv_v), 16)
                ld0.inc(nc.sync.dma_start(out=cos_sb[:, :], in_=cos0), 16)
                ld0.inc(nc.sync.dma_start(out=sin_sb[:, :], in_=sin0), 16)
                nc.vector.memset(onesb[:, :], 1.0)
                nc.vector.memset(onesf[:, :], 1.0)
                ctx.wait(nc.tensor, ld0)
                ctx.wait(nc.vector, ld0)
                ctx.wait(nc.scalar, ld0)
                ctx.bar()
                emit_ln(ctx, x_tiles, xlt, identb, "a_ln")
            lld = ctx.sem("lld")
            lrow = sb(es, nc, "lrow", [1, 16], F32)
            lld.inc(nc.sync.dma_start(out=lrow[:, :], in_=lut_in), 16)
            ctx.wait(nc.tensor, lld)
            with contextlib.ExitStack() as esl:
                lps = psum(esl, nc, "lps", [128, 16])
                lmm = ctx.sem("lmm")
                lmm.inc(nc.tensor.matmul(lps[:, :], onesf[0:1, :], lrow[0:1, :],
                                         start=True, stop=True))
                ctx.wait(nc.vector, lmm)
                ctx.vflush(nc.vector.tensor_copy(lutb1[:, :], lps[:, :]), "lfl")

            ctx.wait(nc.tensor, cld)
            ctx.bar()

            if "qkv" in parts:
              with contextlib.ExitStack() as esq:
                  qbig = sb(esq, nc, "qbig", [128, 32 * T], BF16)
                  with contextlib.ExitStack() as esq2:
                      qlr = sb(esq2, nc, "qlr", [128, 6 * T], BF16)
                      qn_b = sb(esq2, nc, "qn_b", [128, 6 * T], BF16)
                      wb0 = sb(esq2, nc, "qwb0", [128, 16 * 128], BF16)
                      wb1 = sb(esq2, nc, "qwb1", [128, 16 * 128], BF16)
                      with contextlib.ExitStack() as esp:
                          ps0 = psum(esp, nc, "qps0", [128, 512])
                          ps1 = psum(esp, nc, "qps1", [128, 512])

                          def qlr_cb(m, f0, fw, ps, mmn):
                              nc.vector.wait_ge(ctx.sem("mmd_a_qlr").h, mmn)
                              return nc.vector.tensor_copy(qlr[:, m * T + f0:m * T + f0 + fw],
                                                           ps[:, 0:fw])

                          emit_stream_mm(ctx, "a_qlr", plain_pieces(wdq_b.ap(), D), D, Q_LORA,
                                         lambda kc, kk, f0, fw: xlt[:, kc * T + f0:kc * T + f0 + fw],
                                         qlr_cb, [wb0, wb1], [ps0, ps1], n_free=T)
                      emit_rms(ctx, esq2, "a_qrms", qlr, 6, Q_LORA, gq_sb, L0_SCALE, qn_b,
                               onesb, onesf)

                      with contextlib.ExitStack() as esp:
                          ps0 = psum(esp, nc, "qps2", [128, 512])
                          ps1 = psum(esp, nc, "qps3", [128, 512])

                          def qeff_cb(m, f0, fw, ps, mmn):
                              nc.vector.wait_ge(ctx.sem("mmd_a_qe").h, mmn)
                              return nc.vector.tensor_copy(qbig[:, m * T + f0:m * T + f0 + fw],
                                                           ps[:, 0:fw])

                          emit_stream_mm(ctx, "a_qe", plain_pieces(wuq_b.ap(), Q_LORA),
                                         Q_LORA, 4096,
                                         lambda kc, kk, f0, fw: qn_b[:, kc * T + f0:kc * T + f0 + fw],
                                         qeff_cb, [wb0, wb1], [ps0, ps1], n_free=T)

                  o1t = sb(esq, nc, "o1t", [32, 16 * T], BF16)
                  x2c = sb(esq, nc, "x2c", [32, 16 * T], BF16)
                  rtmp = sb(esq, nc, "rtmp", [32, 16 * T], BF16)
                  x1 = qbig[0:32, :].rearrange("p (c t) -> p c t", t=T)[:, 1::2, :]
                  x2src = qbig[32:64, :].rearrange("p (c t) -> p c t", t=T)[:, 1::2, :]
                  x2v = x2c[:, :].rearrange("p (c t) -> p c t", t=T)
                  o1 = o1t[:, :].rearrange("p (c t) -> p c t", t=T)
                  tm = rtmp[:, :].rearrange("p (c t) -> p c t", t=T)
                  cosb = cos_sb[:, :].rearrange("p t -> p () t").broadcast_to([32, NH, T])
                  sinb = sin_sb[:, :].rearrange("p t -> p () t").broadcast_to([32, NH, T])
                  rve = ctx.sem("a_rve")
                  nc.vector.tensor_copy(x2v, x2src)
                  nc.vector.tensor_tensor(o1, x1, cosb, ALU.mult)
                  nc.vector.tensor_tensor(tm, x2v, sinb, ALU.mult)
                  nc.vector.tensor_sub(o1, o1, tm)
                  nc.vector.tensor_tensor(tm, x1, sinb, ALU.mult)
                  nc.vector.tensor_tensor(x2v, x2v, cosb, ALU.mult)
                  rve.inc(nc.vector.tensor_add(tm, tm, x2v))
                  ctx.wait(nc.gpsimd, rve)
                  stq = ctx.sem("a_stq")
                  for h in range(NH):
                      stq.inc(nc.gpsimd.dma_start(
                          out=qn_st[h * NOPE:(h + 1) * NOPE, :],
                          in_=qbig[:, (2 * h) * T:(2 * h + 1) * T]), 16)
                      stq.inc(nc.gpsimd.dma_start(
                          out=qr_st[h * ROPE:h * ROPE + 32, :],
                          in_=o1t[:, h * T:(h + 1) * T]), 16)
                      stq.inc(nc.gpsimd.dma_start(
                          out=qr_st[h * ROPE + 32:(h + 1) * ROPE, :],
                          in_=rtmp[:, h * T:(h + 1) * T]), 16)
                  ctx.wait(nc.gpsimd, stq)
              ctx.bar()

              with contextlib.ExitStack() as esk:
                  kvl = sb(esk, nc, "kvl", [128, 4 * T], BF16)
                  kvn = sb(esk, nc, "kvn", [128, 4 * T], BF16)
                  wb0 = sb(esk, nc, "kwb0", [128, 16 * 128], BF16)
                  wb1 = sb(esk, nc, "kwb1", [128, 16 * 128], BF16)
                  with contextlib.ExitStack() as esp:
                      ps0 = psum(esp, nc, "kps0", [128, 512])
                      ps1 = psum(esp, nc, "kps1", [128, 512])

                      def kv_cb(m, f0, fw, ps, mmn):
                          nc.vector.wait_ge(ctx.sem("mmd_a_kv").h, mmn)
                          return nc.vector.tensor_copy(kvl[:, m * T + f0:m * T + f0 + fw],
                                                       ps[:, 0:fw])

                      emit_stream_mm(ctx, "a_kv", plain_pieces(wdkv_b.ap(), D), D, KV_LORA,
                                     lambda kc, kk, f0, fw: xlt[:, kc * T + f0:kc * T + f0 + fw],
                                     kv_cb, [wb0, wb1], [ps0, ps1], n_free=T)
                  emit_rms(ctx, esk, "a_krms", kvl, 4, KV_LORA, gkv_sb, 1.0, kvn,
                           onesb, onesf)

                  knb = sb(esk, nc, "knb", [128, 16 * T], BF16)
                  with contextlib.ExitStack() as esp:
                      ps0 = psum(esp, nc, "kps2", [128, 512])
                      ps1 = psum(esp, nc, "kps3", [128, 512])

                      def kn_cb(m, f0, fw, ps, mmn):
                          nc.vector.wait_ge(ctx.sem("mmd_a_kn").h, mmn)
                          return nc.vector.tensor_copy(knb[:, m * T + f0:m * T + f0 + fw],
                                                       ps[:, 0:fw])

                      emit_stream_mm(ctx, "a_kn", plain_pieces(wuk_b.ap(), KV_LORA),
                                     KV_LORA, D,
                                     lambda kc, kk, f0, fw: kvn[:, kc * T + f0:kc * T + f0 + fw],
                                     kn_cb, [wb0, wb1], [ps0, ps1], n_free=T)
                  stk = ctx.sem("a_stk")
                  for c in range(16):
                      stk.inc(nc.gpsimd.dma_start(out=kn_st[c * 128:(c + 1) * 128, :],
                                                  in_=knb[:, c * T:(c + 1) * T]), 16)

                  wuv_sb = sb(esk, nc, "wuv_sb", [128, 4 * D], BF16)
                  vld = ctx.sem("a_vld")
                  for kc in range(4):
                      vld.inc(nc.sync.dma_start(out=wuv_sb[:, kc * D:(kc + 1) * D],
                                                in_=wuv_b.ap()[kc * 128:(kc + 1) * 128, :]), 16)
                  ctx.wait(nc.tensor, vld)
                  vrow = sb(esk, nc, "vrow", [128, D], BF16)
                  vmm = ctx.sem("a_vmm")
                  vve = ctx.sem("a_vve")
                  with contextlib.ExitStack() as esp:
                      vps = [psum(esp, nc, "vps%d" % i, [128, 512]) for i in range(2)]
                      for tt in range(8):
                          for dc in range(4):
                              gidx = tt * 4 + dc
                              ps = vps[gidx % 2]
                              if gidx >= 2:
                                  ctx.wait(nc.tensor, vve, gidx - 1)
                              for kc in range(4):
                                  inst = nc.tensor.matmul(
                                      ps[:, :], kvn[:, kc * T + tt * 128:kc * T + (tt + 1) * 128],
                                      wuv_sb[:, kc * D + dc * 512:kc * D + (dc + 1) * 512],
                                      start=(kc == 0), stop=(kc == 3))
                              vmm.inc(inst)
                              ctx.wait(nc.vector, vmm)
                              vve.inc(nc.vector.tensor_copy(vrow[:, dc * 512:(dc + 1) * 512],
                                                            ps[:, :]))
                          ctx.wait(nc.gpsimd, vve)
                          stk.inc(nc.gpsimd.dma_start(out=vt_st[tt * 128:(tt + 1) * 128, :],
                                                      in_=vrow[:, :]), 16)
                          ctx.wait(nc.vector, stk)
                  ctx.wait(nc.gpsimd, stk)

                  kpe = sb(esk, nc, "kpe", [64, T], BF16)
                  kper = sb(esk, nc, "kper", [64, T], BF16)
                  ktmp = sb(esk, nc, "ktmp", [32, T], BF16)
                  with contextlib.ExitStack() as esp:
                      ps0 = psum(esp, nc, "kps4", [64, 512])
                      ps1 = psum(esp, nc, "kps5", [64, 512])

                      def kpe_cb(m, f0, fw, ps, mmn):
                          nc.vector.wait_ge(ctx.sem("mmd_a_kp").h, mmn)
                          return nc.vector.tensor_copy(kpe[:, f0:f0 + fw], ps[0:64, 0:fw])

                      emit_stream_mm(ctx, "a_kp", plain_pieces(wkr_b.ap(), D), D, ROPE,
                                     lambda kc, kk, f0, fw: xlt[:, kc * T + f0:kc * T + f0 + fw],
                                     kpe_cb, [wb0, wb1], [ps0, ps1], n_free=T)
                  kve = ctx.sem("a_kve")
                  kx2 = sb(esk, nc, "kx2", [32, T], BF16)
                  nc.vector.tensor_copy(kx2[:, :], kpe[32:64, :])
                  nc.vector.tensor_tensor(kper[0:32, :], kpe[0:32, :], cos_sb[:, :], ALU.mult)
                  nc.vector.tensor_tensor(ktmp[:, :], kx2[:, :], sin_sb[:, :], ALU.mult)
                  nc.vector.tensor_sub(kper[0:32, :], kper[0:32, :], ktmp[:, :])
                  nc.vector.tensor_tensor(ktmp[:, :], kpe[0:32, :], sin_sb[:, :], ALU.mult)
                  nc.vector.tensor_tensor(kx2[:, :], kx2[:, :], cos_sb[:, :], ALU.mult)
                  nc.vector.tensor_add(ktmp[:, :], ktmp[:, :], kx2[:, :])
                  kve.inc(nc.vector.tensor_copy(kper[32:64, :], ktmp[:, :]))
                  ctx.wait(nc.gpsimd, kve)
                  stk.inc(nc.gpsimd.dma_start(out=kpe_st, in_=kper[:, :]), 16)
                  ctx.wait(nc.gpsimd, stk)
              ctx.bar()
            if "decode" in parts or "dg" in parts:
              with contextlib.ExitStack() as esd:
                  emit_decode(ctx, esd, "dg", idxg, O_PAD, D, lutb1, identb, gate_T,
                              pad_row0=O_PAD - 80)
            if "decode" in parts or "du" in parts:
              with contextlib.ExitStack() as esd:
                  emit_decode(ctx, esd, "du", idxu, O_PAD, D, lutb1, identb, up_T,
                              pad_row0=O_PAD - 80)
            if "decode" in parts or "dd" in parts:
              with contextlib.ExitStack() as esd:
                  emit_decode(ctx, esd, "dd", idxd, D, O_PAD, lutb1, identb, down_T,
                              pad_col0=O_PAD - 80)
            if "absum" in parts:
              with contextlib.ExitStack() as esa:
                  qw = [sb(esa, nc, "aqw%d" % i, [128, D], F32) for i in range(2)]
                  qs = sb(esa, nc, "aqs", [128, 8], F32)
                  qcols = sb(esa, nc, "aqc", [128, 24], F32)
                  ald = ctx.sem("ald")
                  ave = ctx.sem("ave")
                  amm = ctx.sem("amm")
                  ast = ctx.sem("ast")
                  aps = psum(esa, nc, "aps", [1, 2])
                  nc.vector.memset(qs[:, 2:3], 1.0)
                  i = 0
                  for wi, wsl in enumerate((w1s, w2s)):
                      rows, cols = wsl.shape
                      for rt in range(rows // 128):
                          if i >= 2:
                              ctx.wait(nc.sync, ave, i - 1)
                          ald.inc(nc.sync.dma_start(out=qw[i % 2][:, 0:cols],
                                                    in_=wsl[rt * 128:(rt + 1) * 128, :]), 16)
                          ctx.wait(nc.vector, ald)
                          ave.inc(nc.vector.reduce_sum(qcols[:, i:i + 1],
                                                       qw[i % 2][:, 0:cols], AX,
                                                       apply_absolute_value=True))
                          i += 1
                  ave.inc(nc.vector.nop())
                  ctx.wait(nc.vector, ave)
                  ctx.vflush(nc.vector.reduce_sum(qs[:, 0:1], qcols[:, 0:8], AX), "afl")
                  ctx.vflush(nc.vector.reduce_sum(qs[:, 1:2], qcols[:, 8:24], AX), "afl")
                  ave.inc(nc.vector.nop())
                  ctx.wait(nc.tensor, ave)
                  amm.inc(nc.tensor.matmul(aps[0:1, 0:2], qs[:, 2:3], qs[:, 0:2],
                                           start=True, stop=True))
                  ctx.wait(nc.vector, amm)
                  ctx.vflush(nc.vector.tensor_copy(qs[0:1, 4:6], aps[0:1, 0:2]), "afl")
                  ave.inc(nc.vector.nop())
                  ctx.wait(nc.gpsimd, ave)
                  ast.inc(nc.gpsimd.dma_start(out=absum, in_=qs[0:1, 4:6]), 16)
                  ctx.wait(nc.gpsimd, ast)
        ctx.bar()
    return nc


def inputs_A(state):
    full = state["full"]
    ident = np.eye(128, dtype=np.float32).astype(BF)
    w_uq = np.asarray(full["L0_attn_w_uq"], np.float32)
    w_uqp = np.zeros((Q_LORA, 4096), np.float32)
    for h in range(NH):
        blk = w_uq[:, h * QKD:(h + 1) * QKD]
        w_uqp[:, h * 256:h * 256 + 128] = blk[:, :NOPE]
        w_uqp[:, h * 256 + 128:h * 256 + 160] = blk[:, NOPE:NOPE + 32]
        w_uqp[:, h * 256 + 160:h * 256 + 192] = blk[:, NOPE + 32:NOPE + 64]
    gate_idx = np.zeros((NC * O_PAD, D), np.int32)
    up_idx = np.zeros((NC * O_PAD, D), np.int32)
    down_idx = np.zeros((D, NC * O_PAD), np.int32)
    gi = np.asarray(full["L0_ffn_gate_idx"], np.int32)
    ui = np.asarray(full["L0_ffn_up_idx"], np.int32)
    di = np.asarray(full["L0_ffn_down_idx"], np.int32)
    for r in range(NC):
        n = min(688, INTER - r * 688)
        gate_idx[r * O_PAD:r * O_PAD + n] = gi[r * 688:r * 688 + n]
        up_idx[r * O_PAD:r * O_PAD + n] = ui[r * 688:r * 688 + n]
        down_idx[:, r * O_PAD:r * O_PAD + n] = di[:, r * 688:r * 688 + n]
    perm = np.concatenate([np.arange(0, HD, 2), np.arange(1, HD, 2)])
    wq1 = np.asarray(full["L1_attn_wq"], np.float32).reshape(D, NH, HD)[:, :, perm].reshape(D, D)
    wk1 = np.asarray(full["L1_attn_wk"], np.float32).reshape(D, NH, HD)[:, :, perm].reshape(D, D)
    wv1 = np.asarray(full["L1_attn_wv"], np.float32).reshape(D, D)
    wo1 = np.asarray(full["L1_attn_wo"], np.float32).reshape(D, D)
    wo0 = np.asarray(full["L0_attn_w_o"], np.float32)
    inv_freq = 1.0 / (10000.0 ** (np.arange(0, ROPE, 2, dtype=np.float32) / ROPE))
    x = state["x"]
    maps = []
    for r in range(NC):
        pos = np.arange(r * T, (r + 1) * T, dtype=np.float32)
        fr = np.outer(inv_freq, pos)
        m = {
            "x_in": np.ascontiguousarray(x[r * T:(r + 1) * T]),
            "ident": ident,
            "wdq_f": np.asarray(full["L0_attn_w_dq"], np.float32),
            "wuq_f": w_uqp,
            "wdkv_f": np.asarray(full["L0_attn_w_dkv"], np.float32),
            "wkr_f": np.asarray(full["L0_attn_w_kr"], np.float32),
            "wuk_f": np.asarray(full["L0_attn_w_uk"], np.float32),
            "wuv_f": np.asarray(full["L0_attn_w_uv"], np.float32),
            "gq_v": np.ascontiguousarray(np.asarray(full["L0_attn_g_q"], np.float32).reshape(6, 128).T),
            "gkv_v": np.ascontiguousarray(np.asarray(full["L0_attn_g_kv"], np.float32).reshape(4, 128).T),
            "cos0": np.cos(fr).astype(BF),
            "sin0": np.sin(fr).astype(BF),
            "idxg": np.ascontiguousarray(gate_idx[r * O_PAD:(r + 1) * O_PAD]),
            "idxu": np.ascontiguousarray(up_idx[r * O_PAD:(r + 1) * O_PAD]),
            "idxd": np.ascontiguousarray(down_idx[:, r * O_PAD:(r + 1) * O_PAD]),
            "lut_in": np.asarray(full["L0_ffn_lut"], np.float32).reshape(1, 16),
            "w1s": np.ascontiguousarray(np.asarray(full["L1_ffn_w1"], np.float32)[r * 1024:(r + 1) * 1024]),
            "w2s": np.ascontiguousarray(np.asarray(full["L1_ffn_w2"], np.float32)[:, r * 1024:(r + 1) * 1024]),
            "wo0_s": np.ascontiguousarray(wo0[r * 256:(r + 1) * 256]),
            "wq1_s": np.ascontiguousarray(wq1[r * 256:(r + 1) * 256]),
            "wk1_s": np.ascontiguousarray(wk1[r * 256:(r + 1) * 256]),
            "wv1_s": np.ascontiguousarray(wv1[r * 256:(r + 1) * 256]),
            "wo1_s": np.ascontiguousarray(wo1[r * 256:(r + 1) * 256]),
        }
        maps.append(m)
    return maps


def run_A(state):
    nc = build_A()
    res = run_bass_kernel_spmd(nc, inputs_A(state), core_ids=list(range(NC)))
    return res.results



def emit_quantize(ctx, es, w1s, w2s, absums_in, w1q_st, w2q_st, ident_bf,
                  onesf):
    nc = ctx.nc
    ld = ctx.sem("quld")
    ve = ctx.sem("quve")
    mm = ctx.sem("qumm")
    st = ctx.sem("qust")
    tse = ctx.sem("quts")
    with contextlib.ExitStack() as es2:
        qw = sb(es2, nc, "quw", [128, D], F32)
        qs = sb(es2, nc, "qus", [128, 8], F32)
        arow = sb(es2, nc, "quar", [1, 16], F32)
        qbf = sb(es2, nc, "qubf", [128, D], BF16)
        qtr = sb(es2, nc, "qutr", [128, D], BF16)
        qps = psum(es2, nc, "qups", [128, 8])
        tpb = [psum(es2, nc, "qutp0", [128, 512], BF16),
               psum(es2, nc, "qutp1", [128, 512], BF16)]
        ld.inc(nc.sync.dma_start(out=arow[:, :], in_=absums_in), 16)
        ctx.wait(nc.vector, ld)
        ctx.vflush(nc.vector.reduce_sum(qs[0:1, 4:5], arow[0:1, 0:8], AX), "qufl")
        ctx.vflush(nc.vector.reduce_sum(qs[0:1, 5:6], arow[0:1, 8:16], AX), "qufl")
        ctx.vflush(nc.vector.tensor_scalar(qs[0:1, 4:6], qs[0:1, 4:6],
                                           1.0 / (DFF * D), 1e-8, ALU.mult, ALU.max), "qufl")
        ctx.vflush(nc.vector.reciprocal(qs[0:1, 4:6], qs[0:1, 4:6]), "qufl")
        ve.inc(nc.vector.nop())
        ctx.wait(nc.tensor, ve)
        mm.inc(nc.tensor.matmul(qps[:, 4:6], onesf[0:1, :], qs[0:1, 4:6],
                                start=True, stop=True))
        ctx.wait(nc.vector, mm)
        ctx.vflush(nc.vector.tensor_copy(qs[:, 4:6], qps[:, 4:6]), "qufl")
        ve.inc(nc.vector.nop())
        tb = 0
        for wi, (wsl, stage) in enumerate(((w1s, w1q_st), (w2s, w2q_st))):
            rows, cols = wsl.shape
            for rt in range(rows // 128):
                if not (wi == 0 and rt == 0):
                    ctx.wait(nc.sync, st)
                ld.inc(nc.sync.dma_start(out=qw[:, 0:cols],
                                         in_=wsl[rt * 128:(rt + 1) * 128, :]), 16)
                ctx.wait(nc.vector, ld)
                ctx.wait(nc.vector, st)
                nc.vector.tensor_scalar(qw[:, 0:cols], qw[:, 0:cols],
                                        qs[:, 4 + wi:5 + wi], None, ALU.mult)
                nc.vector.tensor_scalar(qw[:, 0:cols], qw[:, 0:cols], RBIG, RBIG,
                                        ALU.add, ALU.subtract)
                ve.inc(nc.vector.tensor_scalar(qbf[:, 0:cols], qw[:, 0:cols],
                                               1.0, -1.0, ALU.min, ALU.max))
                ctx.wait(nc.tensor, ve)
                for g in range(cols // 512):
                    tp = tpb[tb % 2]
                    if tb >= 2:
                        ctx.wait(nc.tensor, tse, tb - 1)
                    for c4 in range(4):
                        c = g * 4 + c4
                        inst = nc.tensor.transpose(tp[:, c4 * 128:(c4 + 1) * 128],
                                                   qbf[:, c * 128:(c + 1) * 128],
                                                   ident_bf[:, :])
                    mm.inc(inst)
                    ctx.wait(nc.vector, mm)
                    for c4 in range(4):
                        c = g * 4 + c4
                        inst = nc.vector.tensor_copy(qtr[:, c * 128:(c + 1) * 128],
                                                     tp[:, c4 * 128:(c4 + 1) * 128])
                    tse.inc(inst)
                    tb += 1
                ctx.wait(nc.gpsimd, tse)
                for c in range(cols // 128):
                    st.inc(nc.gpsimd.dma_start(
                        out=stage[c * 128:(c + 1) * 128, rt * 128:(rt + 1) * 128],
                        in_=qtr[:, c * 128:(c + 1) * 128]), 16)
        ctx.wait(nc.gpsimd, st)
    ctx.bar()


def emit_attn_head(ctx, es, hl, k_main, km_rows, k_aug, q_main, qm_rows, q_aug,
                   v_sb, ssq_kpe, masks, attn_st, onesb, onesf, sqs, rowA, rowB,
                   skip=()):
    nc = ctx.nc
    tg = "ah%d" % hl
    se = ctx.sem("se_" + tg)
    ve = ctx.sem("ve_" + tg)
    mm = ctx.sem("mm_" + tg)
    s_p = ctx.sem("sp_" + tg)
    s_pm = ctx.sem("spm_" + tg)
    s_pv = ctx.sem("spv_" + tg)
    s_nm = ctx.sem("snm_" + tg)
    stq = ctx.sem("stq_" + tg)

    _c = [0]
    if "norms" in skip:
        nc.vector.memset(q_aug[64:65, :], -10.0)

    def ssq_pass(main_ap, main_rows, aug_ap, out_row):
        _c[0] += 1
        with contextlib.ExitStack() as esn:
            nps = psum(esn, nc, "nps%d_" % _c[0] + tg, [1, 512])
            sq64 = sb(esn, nc, "sq64%d_" % _c[0] + tg, [64, 512], BF16)
            se.inc(nc.scalar.activation(sqs[0:main_rows, :], main_ap, AF.Square))
            for sl in range(16):
                if aug_ap is not None:
                    ctx.wait(nc.scalar, mm)
                    se.inc(nc.scalar.activation(
                        sq64[:, :], aug_ap[0:64, sl * 512:(sl + 1) * 512], AF.Square))
                ctx.wait(nc.tensor, se)
                inst = nc.tensor.matmul(nps[0:1, :], onesb[0:main_rows, 0:1],
                                        sqs[0:main_rows, sl * 512:(sl + 1) * 512],
                                        start=True, stop=(aug_ap is None))
                if aug_ap is not None:
                    inst = nc.tensor.matmul(nps[0:1, :], onesb[0:64, 0:1], sq64[:, :],
                                            start=False, stop=True)
                mm.inc(inst)
                ctx.wait(nc.vector, mm)
                ve.inc(nc.vector.tensor_copy(out_row[0:1, sl * 512:(sl + 1) * 512],
                                             nps[0:1, :]))
                ctx.wait(nc.tensor, ve)

    if "norms" not in skip:
        ssq_pass(k_main[:, :], km_rows, None if ssq_kpe is not None else k_aug, rowA)
        if ssq_kpe is not None:
            ctx.wait(nc.vector, ve)
            ctx.vflush(nc.vector.tensor_add(rowA[0:1, :], rowA[0:1, :],
                                            ssq_kpe[0:1, :]), "nfl_" + tg)
        ctx.vflush(nc.vector.reduce_max(rowB[0:1, 0:1], rowA[0:1, :], AX), "nfl_" + tg)
        ve.inc(nc.vector.nop())
        ctx.wait(nc.scalar, ve)
        se.inc(nc.scalar.activation(rowB[0:1, 0:1], rowB[0:1, 0:1], AF.Sqrt))
        ssq_pass(q_main[:, :], qm_rows, q_aug, rowA)
        ctx.wait(nc.scalar, ve)
        se.inc(nc.scalar.activation(rowA[0:1, :], rowA[0:1, :], AF.Sqrt))
        ctx.wait(nc.vector, se)
        ctx.vflush(nc.vector.tensor_scalar(rowA[0:1, :], rowA[0:1, :],
                                           rowB[0:1, 0:1], -1.0, ALU.mult, ALU.mult),
                   "nfl_" + tg)
        ve.inc(nc.vector.tensor_copy(q_aug[64:65, :], rowA[0:1, :]))
        ctx.wait(nc.tensor, ve)
    ctx.bar()

    if "flash" in skip:
        return
    NS = 3
    with contextlib.ExitStack() as esf:
        scp = [psum(esf, nc, "scp%d_" % i + tg, [128, 512]) for i in range(NS)]
        pvp = psum(esf, nc, "pvp_" + tg, [128, 512])
        smp = psum(esf, nc, "smp_" + tg, [1, 512])
        bcp = psum(esf, nc, "bcp_" + tg, [128, 512])
        psb = [sb(esf, nc, "psb%d_" % i + tg, [128, 512], BF16) for i in range(NS)]
        srow = sb(esf, nc, "srow_" + tg, [1, 512], F32)
        rcb = sb(esf, nc, "rcb_" + tg, [128, 512], BF16)
        att = sb(esf, nc, "att_" + tg, [128, 512], BF16)
        for j in range(16):
            nk = 4 * j + 4
            pend = []
            for k in range(nk + 1):
                if k < nk:
                    diag = k >= 4 * j
                    if k >= NS:
                        ctx.wait(nc.tensor, s_p, s_p.n - (NS - 1))
                    nc.tensor.matmul(scp[k % NS][:, :],
                                     k_main[:, k * 128:(k + 1) * 128],
                                     q_main[:, j * 512:(j + 1) * 512],
                                     start=True, stop=False)
                    mm.inc(nc.tensor.matmul(scp[k % NS][:, :],
                                            k_aug[:, k * 128:(k + 1) * 128],
                                            q_aug[:, j * 512:(j + 1) * 512],
                                            start=False, stop=True))
                    nc.scalar.wait_ge(mm.h, mm.n)
                    if k >= NS:
                        nc.scalar.wait_ge(s_pv.h, s_pv.n)
                    s_p.inc(nc.scalar.activation(psb[k % NS][:, :],
                                                 scp[k % NS][:, :], AF.Exp))
                    if diag:
                        c4 = k - 4 * j
                        nc.vector.wait_ge(s_p.h, s_p.n)
                        s_pm.inc(nc.vector.tensor_tensor(
                            psb[k % NS][:, :], psb[k % NS][:, :],
                            masks[:, c4 * 512:(c4 + 1) * 512], ALU.mult))
                    pend.append((k, diag, s_p.n, s_pm.n))
                while pend and (len(pend) >= NS or k == nk):
                    pk, pdiag, spn, spmn = pend.pop(0)
                    if pdiag:
                        ctx.wait(nc.tensor, s_pm, spmn)
                    else:
                        ctx.wait(nc.tensor, s_p, spn)
                    if pk == 0:
                        ctx.wait(nc.tensor, s_nm)
                    nc.tensor.matmul(pvp[:, :], v_sb[:, pk * 128:(pk + 1) * 128],
                                     psb[pk % NS][:, :], start=(pk == 0),
                                     stop=(pk == nk - 1), skip_group_check=True)
                    s_pv.inc(nc.tensor.matmul(smp[0:1, :], onesb[:, 0:1],
                                              psb[pk % NS][:, :], start=(pk == 0),
                                              stop=(pk == nk - 1),
                                              skip_group_check=True))
            ctx.wait(nc.vector, s_pv)
            ctx.vflush(nc.vector.reciprocal(srow[0:1, :], smp[0:1, :]),
                       "rfl_" + tg)
            ve.inc(nc.vector.nop())
            ctx.wait(nc.tensor, ve)
            mm.inc(nc.tensor.matmul(bcp[:, :], onesf[0:1, :], srow[0:1, :],
                                    start=True, stop=True))
            ctx.wait(nc.vector, mm)
            nc.vector.tensor_copy(rcb[:, :], bcp[:, :])
            if j > 0:
                ctx.wait(nc.vector, stq)
            s_nm.inc(nc.vector.tensor_tensor(att[:, :], pvp[:, :], rcb[:, :],
                                             ALU.mult))
            s_nm.inc(nc.vector.nop())
            ctx.wait(nc.gpsimd, s_nm)
            stq.inc(nc.gpsimd.dma_start(
                out=attn_st[hl * 128:(hl + 1) * 128, j * 512:(j + 1) * 512],
                in_=att[:, :]), 16)
        ctx.wait(nc.gpsimd, stq)
    ctx.bar()


def build_B(skip=()):
    nc = bass.Bass("TRN2", target_bir_lowering=False)
    ctx = Ctx(nc)
    kn_in = nc.dram_tensor("kn_in", [2 * NOPE, S], BF16, kind="ExternalInput").ap()
    vt_in = nc.dram_tensor("vt_in", [S, 2 * VD], BF16, kind="ExternalInput").ap()
    kpe_in = nc.dram_tensor("kpe_in", [ROPE, S], BF16, kind="ExternalInput").ap()
    qn_in = nc.dram_tensor("qn_in", [2 * NOPE, S], BF16, kind="ExternalInput").ap()
    qr_in = nc.dram_tensor("qr_in", [2 * ROPE, S], BF16, kind="ExternalInput").ap()
    masks_in = nc.dram_tensor("masks_in", [128, 4 * 512], BF16, kind="ExternalInput").ap()
    absums_in = nc.dram_tensor("absums_in", [1, 16], F32, kind="ExternalInput").ap()
    w1s = nc.dram_tensor("w1s", [DFF // NC, D], F32, kind="ExternalInput").ap()
    w2s = nc.dram_tensor("w2s", [D, DFF // NC], F32, kind="ExternalInput").ap()
    ident_in = nc.dram_tensor("ident", [128, 128], BF16, kind="ExternalInput").ap()

    attn_st = nc.dram_tensor("attn_st", [2 * VD, S], BF16, kind="ExternalOutput").ap()
    w1q_st = nc.dram_tensor("w1q_st", [D, T], BF16, kind="ExternalOutput").ap()
    w2q_st = nc.dram_tensor("w2q_st", [T, D], BF16, kind="ExternalOutput").ap()

    with contextlib.ExitStack() as es:
        es.enter_context(nc.Block())
        identb = sb(es, nc, "identb", [128, 128], BF16)
        onesb = sb(es, nc, "onesb", [128, 1], BF16)
        onesf = sb(es, nc, "onesf", [1, 128], F32)
        masks = sb(es, nc, "masks", [128, 4 * 512], BF16)
        kr_sb = sb(es, nc, "kr_sb", [65, S], BF16)
        ssq_kpe = sb(es, nc, "ssq_kpe", [1, S], F32)
        ld0 = ctx.sem("ld0")
        ld0.inc(nc.sync.dma_start(out=identb[:, :], in_=ident_in), 16)
        ld0.inc(nc.sync.dma_start(out=masks[:, :], in_=masks_in), 16)
        ld0.inc(nc.sync.dma_start(out=kr_sb[0:64, :], in_=kpe_in), 16)
        nc.vector.memset(onesb[:, :], 1.0)
        nc.vector.memset(onesf[:, :], 1.0)
        ctx.wait(nc.vector, ld0)
        nc.vector.memset(kr_sb[64:65, :], 1.0)
        ctx.wait(nc.tensor, ld0)
        ctx.bar()

        emit_quantize(ctx, es, w1s, w2s, absums_in, w1q_st, w2q_st, identb, onesf)

        with contextlib.ExitStack() as esn:
            sqk = sb(esn, nc, "sqk", [64, S], BF16)
            nps = psum(esn, nc, "npsk", [1, 512])
            kse = ctx.sem("kse")
            kmm = ctx.sem("kmm")
            kve = ctx.sem("kve")
            kse.inc(nc.scalar.activation(sqk[:, :], kr_sb[0:64, :], AF.Square))
            ctx.wait(nc.tensor, kse)
            for sl in range(16):
                kmm.inc(nc.tensor.matmul(nps[0:1, :], onesb[0:64, 0:1],
                                         sqk[:, sl * 512:(sl + 1) * 512],
                                         start=True, stop=True))
                ctx.wait(nc.vector, kmm)
                kve.inc(nc.vector.tensor_copy(ssq_kpe[0:1, sl * 512:(sl + 1) * 512],
                                              nps[0:1, :]))
                ctx.wait(nc.tensor, kve)
        ctx.bar()

        for hl in range(2):
            with contextlib.ExitStack() as eh:
                kn_h = sb(eh, nc, "knh%d" % hl, [128, S], BF16)
                v_sb = sb(eh, nc, "vsb%d" % hl, [128, 64 * 128], BF16)
                qn_h = sb(eh, nc, "qnh%d" % hl, [128, S], BF16)
                qraug = sb(eh, nc, "qra%d" % hl, [65, S], BF16)
                sqs = sb(eh, nc, "sqs%d" % hl, [128, S], BF16)
                rowA = sb(eh, nc, "rwa%d" % hl, [1, S], F32)
                rowB = sb(eh, nc, "rwb%d" % hl, [1, 16], F32)
                hld = ctx.sem("hld%d" % hl)
                hld.inc(nc.sync.dma_start(out=kn_h[:, :],
                                          in_=kn_in[hl * 128:(hl + 1) * 128, :]), 16)
                hld.inc(nc.sync.dma_start(out=qn_h[:, :],
                                          in_=qn_in[hl * 128:(hl + 1) * 128, :]), 16)
                hld.inc(nc.sync.dma_start(out=qraug[0:64, :],
                                          in_=qr_in[hl * 64:(hl + 1) * 64, :]), 16)
                vsrc = vt_in[:, hl * 128:(hl + 1) * 128].rearrange(
                    "(b p) d -> p b d", p=128)
                vdst = v_sb[:, :].rearrange("p (b d) -> p b d", d=128)
                hld.inc(nc.sync.dma_start(out=vdst, in_=vsrc), 16)
                ctx.wait(nc.tensor, hld)
                ctx.wait(nc.scalar, hld)
                ctx.wait(nc.vector, hld)
                ctx.bar()
                emit_attn_head(ctx, eh, hl, kn_h, 128, kr_sb, qn_h, 128, qraug,
                               v_sb, ssq_kpe, masks, attn_st, onesb, onesf, sqs,
                               rowA, rowB, skip=skip)
    return nc


def inputs_B(state, layer=0):
    full = state["full"]
    ident = np.eye(128, dtype=np.float32).astype(BF)
    masks = np.zeros((128, 4, 512), np.float32)
    tk = np.arange(128)[:, None]
    tq = np.arange(512)[None, :]
    for ci, c in enumerate((0, 128, 256, 384)):
        masks[:, ci, :] = (tq >= tk + c)
    masks = masks.reshape(128, 2048).astype(BF)
    kn = state["kn_full"]
    vt = state["vt_full"]
    qn = state["qn_full"]
    qr = state["qr_full"]
    kpe = state.get("kpe_full")
    maps = []
    for r in range(NC):
        m = {
            "ident": ident, "masks_in": masks,
            "kn_in": np.ascontiguousarray(kn[r * 256:(r + 1) * 256]),
            "vt_in": np.ascontiguousarray(vt[:, r * 256:(r + 1) * 256]),
            "qn_in": np.ascontiguousarray(qn[r * 256:(r + 1) * 256]),
            "qr_in": np.ascontiguousarray(qr[r * 128:(r + 1) * 128]),
            "kpe_in": np.ascontiguousarray(kpe),
            "absums_in": state["absums"],
            "w1s": np.ascontiguousarray(np.asarray(full["L1_ffn_w1"], np.float32)[r * 1024:(r + 1) * 1024]),
            "w2s": np.ascontiguousarray(np.asarray(full["L1_ffn_w2"], np.float32)[:, r * 1024:(r + 1) * 1024]),
        }
        maps.append(m)
    return maps


def run_B(state):
    nc = build_B()
    res = run_bass_kernel_spmd(nc, inputs_B(state), core_ids=list(range(NC)))
    return res.results



def build_D():
    nc = bass.Bass("TRN2", target_bir_lowering=False)
    ctx = Ctx(nc)
    k1_in = nc.dram_tensor("k1_in", [128, S], BF16, kind="ExternalInput").ap()
    k2_in = nc.dram_tensor("k2_in", [128, S], BF16, kind="ExternalInput").ap()
    q1_in = nc.dram_tensor("q1_in", [128, S], BF16, kind="ExternalInput").ap()
    q2_in = nc.dram_tensor("q2_in", [128, S], BF16, kind="ExternalInput").ap()
    vt_in = nc.dram_tensor("vt_in", [S, 2 * VD], BF16, kind="ExternalInput").ap()
    masks_in = nc.dram_tensor("masks_in", [128, 4 * 512], BF16, kind="ExternalInput").ap()
    ident_in = nc.dram_tensor("ident", [128, 128], BF16, kind="ExternalInput").ap()
    attn_st = nc.dram_tensor("attn_st", [2 * VD, S], BF16, kind="ExternalOutput").ap()

    with contextlib.ExitStack() as es:
        es.enter_context(nc.Block())
        identb = sb(es, nc, "identb", [128, 128], BF16)
        onesb = sb(es, nc, "onesb", [128, 1], BF16)
        onesf = sb(es, nc, "onesf", [1, 128], F32)
        masks = sb(es, nc, "masks", [128, 4 * 512], BF16)
        ld0 = ctx.sem("ld0")
        ld0.inc(nc.sync.dma_start(out=identb[:, :], in_=ident_in), 16)
        ld0.inc(nc.sync.dma_start(out=masks[:, :], in_=masks_in), 16)
        nc.vector.memset(onesb[:, :], 1.0)
        nc.vector.memset(onesf[:, :], 1.0)
        ctx.wait(nc.tensor, ld0)
        ctx.wait(nc.vector, ld0)
        ctx.bar()
        for hl in range(2):
            with contextlib.ExitStack() as eh:
                k2_h = sb(eh, nc, "k2h%d" % hl, [64, S], BF16)
                k1s = sb(eh, nc, "k1s%d" % hl, [65, S], BF16)
                q2_h = sb(eh, nc, "q2h%d" % hl, [64, S], BF16)
                q1aug = sb(eh, nc, "q1a%d" % hl, [65, S], BF16)
                v_sb = sb(eh, nc, "vsb%d" % hl, [128, 64 * 128], BF16)
                sqs = sb(eh, nc, "sqs%d" % hl, [128, S], BF16)
                rowA = sb(eh, nc, "rwa%d" % hl, [1, S], F32)
                rowB = sb(eh, nc, "rwb%d" % hl, [1, 16], F32)
                hld = ctx.sem("hld%d" % hl)
                hld.inc(nc.sync.dma_start(out=k2_h[:, :],
                                          in_=k2_in[hl * 64:(hl + 1) * 64, :]), 16)
                hld.inc(nc.sync.dma_start(out=k1s[0:64, :],
                                          in_=k1_in[hl * 64:(hl + 1) * 64, :]), 16)
                hld.inc(nc.sync.dma_start(out=q2_h[:, :],
                                          in_=q2_in[hl * 64:(hl + 1) * 64, :]), 16)
                hld.inc(nc.sync.dma_start(out=q1aug[0:64, :],
                                          in_=q1_in[hl * 64:(hl + 1) * 64, :]), 16)
                vsrc = vt_in[:, hl * 128:(hl + 1) * 128].rearrange(
                    "(b p) d -> p b d", p=128)
                vdst = v_sb[:, :].rearrange("p (b d) -> p b d", d=128)
                hld.inc(nc.sync.dma_start(out=vdst, in_=vsrc), 16)
                ctx.wait(nc.vector, hld)
                nc.vector.memset(k1s[64:65, :], 1.0)
                ctx.wait(nc.tensor, hld)
                ctx.wait(nc.scalar, hld)
                ctx.bar()
                emit_attn_head(ctx, eh, hl, k2_h, 64, k1s, q2_h, 64, q1aug,
                               v_sb, None, masks, attn_st, onesb, onesf, sqs,
                               rowA, rowB)
    return nc


def inputs_D(state):
    ident = np.eye(128, dtype=np.float32).astype(BF)
    masks = np.zeros((128, 4, 512), np.float32)
    tk = np.arange(128)[:, None]
    tq = np.arange(512)[None, :]
    for ci, c in enumerate((0, 128, 256, 384)):
        masks[:, ci, :] = (tq >= tk + c)
    masks = masks.reshape(128, 2048).astype(BF)
    maps = []
    for r in range(NC):
        maps.append({
            "ident": ident, "masks_in": masks,
            "k1_in": np.ascontiguousarray(state["k1_full"][r * 128:(r + 1) * 128]),
            "k2_in": np.ascontiguousarray(state["k2_full"][r * 128:(r + 1) * 128]),
            "q1_in": np.ascontiguousarray(state["q1_full"][r * 128:(r + 1) * 128]),
            "q2_in": np.ascontiguousarray(state["q2_full"][r * 128:(r + 1) * 128]),
            "vt_in": np.ascontiguousarray(state["vt1_full"][:, r * 256:(r + 1) * 256]),
        })
    return maps


def run_D(state):
    nc = build_D()
    res = run_bass_kernel_spmd(nc, inputs_D(state), core_ids=list(range(NC)))
    return res.results



def build_C():
    nc = bass.Bass("TRN2", target_bir_lowering=False)
    ctx = Ctx(nc)
    x_in = nc.dram_tensor("x_in", [T, D], F32, kind="ExternalInput").ap()
    x_out = nc.dram_tensor("x_out", [T, D], F32, kind="ExternalOutput").ap()
    ident_in = nc.dram_tensor("ident", [128, 128], BF16, kind="ExternalInput").ap()
    attn_in = nc.dram_tensor("attn_in", [D, T], BF16, kind="ExternalInput").ap()
    wo_in = nc.dram_tensor("wo_in", [D, D], BF16, kind="ExternalInput").ap()
    gate_in = nc.dram_tensor("gate_in", [D, NC * O_PAD], BF16, kind="ExternalInput").ap()
    up_in = nc.dram_tensor("up_in", [D, NC * O_PAD], BF16, kind="ExternalInput").ap()
    down_in = nc.dram_tensor("down_in", [NC * O_PAD, D], BF16, kind="ExternalInput").ap()
    wq_in = nc.dram_tensor("wq_in", [D, D], BF16, kind="ExternalInput").ap()
    wk_in = nc.dram_tensor("wk_in", [D, D], BF16, kind="ExternalInput").ap()
    wv_in = nc.dram_tensor("wv_in", [D, D], BF16, kind="ExternalInput").ap()
    cos1q = nc.dram_tensor("cos1q", [64, T], BF16, kind="ExternalInput").ap()
    sin1q = nc.dram_tensor("sin1q", [64, T], BF16, kind="ExternalInput").ap()
    cos1k = nc.dram_tensor("cos1k", [64, T], BF16, kind="ExternalInput").ap()
    sin1k = nc.dram_tensor("sin1k", [64, T], BF16, kind="ExternalInput").ap()

    q1_st = nc.dram_tensor("q1_st", [NH * 64, T], BF16, kind="ExternalOutput").ap()
    q2_st = nc.dram_tensor("q2_st", [NH * 64, T], BF16, kind="ExternalOutput").ap()
    k1_st = nc.dram_tensor("k1_st", [NH * 64, T], BF16, kind="ExternalOutput").ap()
    k2_st = nc.dram_tensor("k2_st", [NH * 64, T], BF16, kind="ExternalOutput").ap()
    vt1_st = nc.dram_tensor("vt1_st", [T, D], BF16, kind="ExternalOutput").ap()

    with contextlib.ExitStack() as es:
        x_tiles = [sb(es, nc, "x%d" % t, [128, D], F32) for t in range(8)]
        identb = sb(es, nc, "identb", [128, 128], BF16)
        onesb = sb(es, nc, "onesb", [128, 1], BF16)
        onesf = sb(es, nc, "onesf", [1, 128], F32)
        es.enter_context(nc.Block())
        ld0 = ctx.sem("ld0")
        for t in range(8):
            ld0.inc(nc.sync.dma_start(out=x_tiles[t][:, :],
                                      in_=x_in[t * 128:(t + 1) * 128, :]), 16)
        ld0.inc(nc.sync.dma_start(out=identb[:, :], in_=ident_in), 16)
        nc.vector.memset(onesb[:, :], 1.0)
        nc.vector.memset(onesf[:, :], 1.0)

        with contextlib.ExitStack() as es2:
            atT = sb(es2, nc, "atT", [128, 16 * T], BF16)
            woS = sb(es2, nc, "woT", [128, 16 * D], BF16)
            for c in range(16):
                ld0.inc(nc.sync.dma_start(out=atT[:, c * T:(c + 1) * T],
                                          in_=attn_in[c * 128:(c + 1) * 128, :]), 16)
                ld0.inc(nc.sync.dma_start(out=woS[:, c * D:(c + 1) * D],
                                          in_=wo_in[c * 128:(c + 1) * 128, :]), 16)
            ctx.wait(nc.tensor, ld0)
            ctx.wait(nc.vector, ld0)
            ctx.bar()
            emit_token_proj(ctx, "c_ao",
                            lambda kc, tg: atT[:, kc * T + tg * 128:kc * T + (tg + 1) * 128],
                            16, [woS[:, c * D:(c + 1) * D] for c in range(16)],
                            x_tiles)

        with contextlib.ExitStack() as es3:
            xlt = sb(es3, nc, "xlt", [128, 16 * T], BF16)
            emit_ln(ctx, x_tiles, xlt, identb, "c_ln1")
            NOC = NC * O_PAD // 128
            for th in range(2):
                tq0 = th * 512
                with contextlib.ExitStack() as es4:
                    gsil = sb(es4, nc, "gsil%d" % th, [128, NOC * 512], BF16)
                    hT = sb(es4, nc, "hT%d" % th, [128, NOC * 512], BF16)
                    with contextlib.ExitStack() as es5:
                        wb0 = sb(es5, nc, "cwb0_%d" % th, [128, 16 * 128], BF16)
                        wb1 = sb(es5, nc, "cwb1_%d" % th, [128, 16 * 128], BF16)
                        ps0 = psum(es5, nc, "cps0_%d" % th, [128, 512])
                        ps1 = psum(es5, nc, "cps1_%d" % th, [128, 512])

                        def g_cb(m, f0, fw, ps, mmn, _th=th):
                            nc.scalar.wait_ge(ctx.sem("mmd_c_g%d" % _th).h, mmn)
                            return nc.scalar.activation(
                                gsil[:, m * 512:m * 512 + fw], ps[:, 0:fw], AF.Silu)

                        emit_stream_mm(ctx, "c_g%d" % th, plain_pieces(gate_in, D),
                                       D, NC * O_PAD,
                                       lambda kc, kk, f0, fw, _tq0=tq0: xlt[:, kc * T + _tq0 + f0:kc * T + _tq0 + f0 + fw],
                                       g_cb, [wb0, wb1], [ps0, ps1], n_free=512)

                        def u_cb(m, f0, fw, ps, mmn, _th=th):
                            nc.vector.wait_ge(ctx.sem("mmd_c_u%d" % _th).h, mmn)
                            return nc.vector.tensor_tensor(
                                hT[:, m * 512:m * 512 + fw], ps[:, 0:fw],
                                gsil[:, m * 512:m * 512 + fw], ALU.mult)

                        emit_stream_mm(ctx, "c_u%d" % th, plain_pieces(up_in, D),
                                       D, NC * O_PAD,
                                       lambda kc, kk, f0, fw, _tq0=tq0: xlt[:, kc * T + _tq0 + f0:kc * T + _tq0 + f0 + fw],
                                       u_cb, [wb0, wb1], [ps0, ps1], n_free=512)

                    tag2 = "c_d%d" % th
                    dld = ctx.sem("dld_" + tag2)
                    dfr = ctx.sem("dfr_" + tag2)
                    dmm = ctx.sem("dmm_" + tag2)
                    dve = ctx.sem("dve_" + tag2)
                    with contextlib.ExitStack() as es6:
                        db = [sb(es6, nc, "cdb%d_%d" % (i, th), [128, D], BF16)
                              for i in range(3)]
                        ops = [psum(es6, nc, "cop%d_%d" % (i, th), [128, 512])
                               for i in range(8)]

                        def d_load(oc):
                            dld.inc(nc.sync.dma_start(
                                out=db[oc % 2][:, :],
                                in_=down_in[oc * 128:(oc + 1) * 128, :]), 16)

                        for dch in range(2):
                            ctx.wait(nc.sync, dfr)
                            for ff in range(2):
                                d_load(ff)
                            nld = 2
                            for oc in range(NOC):
                                ctx.wait(nc.tensor, dld, 16 * (oc + 1) + 16 * NOC * dch)
                                for tg in range(4):
                                    for d2 in range(2):
                                        inst = nc.tensor.matmul(
                                            ops[tg * 2 + d2][:, :],
                                            hT[:, oc * 512 + tg * 128:oc * 512 + (tg + 1) * 128],
                                            db[oc % 2][:, dch * 1024 + d2 * 512:
                                                       dch * 1024 + (d2 + 1) * 512],
                                            start=(oc == 0), stop=(oc == NOC - 1))
                                dfr.inc(inst)
                                if nld < NOC:
                                    ctx.wait(nc.sync, dfr, dfr.n)
                                    d_load(nld)
                                    nld += 1
                            ctx.wait(nc.vector, dfr)
                            for tg in range(4):
                                xt = x_tiles[th * 4 + tg]
                                for d2 in range(2):
                                    inst = nc.vector.scalar_tensor_tensor(
                                        xt[:, dch * 1024 + d2 * 512:dch * 1024 + (d2 + 1) * 512],
                                        ops[tg * 2 + d2][:, :], 1.0,
                                        xt[:, dch * 1024 + d2 * 512:dch * 1024 + (d2 + 1) * 512],
                                        ALU.mult, ALU.add)
                            dve.inc(inst)
                            ctx.wait(nc.tensor, dve)
                        ctx.bar()

        with contextlib.ExitStack() as es7:
            xlt2 = sb(es7, nc, "xlt2", [128, 16 * T], BF16)
            emit_ln(ctx, x_tiles, xlt2, identb, "c_ln2")
            rld = ctx.sem("rld")
            ctabs = {}
            for nm2, apz in (("cq", cos1q), ("sq", sin1q), ("ck", cos1k), ("sk", sin1k)):
                ctabs[nm2] = sb(es7, nc, "tab_" + nm2, [64, T], BF16)
                rld.inc(nc.sync.dma_start(out=ctabs[nm2][:, :], in_=apz), 16)
            ctx.wait(nc.vector, rld)

            def qk_pass(tag3, w_ap, cosT, sinT, st1, st2):
                with contextlib.ExitStack() as es8:
                    qb = sb(es8, nc, "qb_" + tag3, [128, 16 * T], BF16)
                    with contextlib.ExitStack() as es9:
                        wb0 = sb(es9, nc, "wb0_" + tag3, [128, 16 * 128], BF16)
                        wb1 = sb(es9, nc, "wb1_" + tag3, [128, 16 * 128], BF16)
                        ps0 = psum(es9, nc, "ps0_" + tag3, [128, 512])
                        ps1 = psum(es9, nc, "ps1_" + tag3, [128, 512])

                        def cb(m, f0, fw, ps, mmn):
                            nc.vector.wait_ge(ctx.sem("mmd_" + tag3).h, mmn)
                            return nc.vector.tensor_copy(
                                qb[:, m * T + f0:m * T + f0 + fw], ps[:, 0:fw])

                        emit_stream_mm(ctx, tag3, plain_pieces(w_ap, D), D, D,
                                       lambda kc, kk, f0, fw: xlt2[:, kc * T + f0:kc * T + f0 + fw],
                                       cb, [wb0, wb1], [ps0, ps1], n_free=T)
                    o1t = sb(es8, nc, "o1_" + tag3, [64, 4 * T], BF16)
                    x2c = sb(es8, nc, "x2_" + tag3, [64, 4 * T], BF16)
                    rtm = sb(es8, nc, "rt_" + tag3, [64, 4 * T], BF16)
                    rve = ctx.sem("rv_" + tag3)
                    rst = ctx.sem("rs_" + tag3)
                    for g4 in range(4):
                        if g4 > 0:
                            ctx.wait(nc.vector, rst)
                        x1 = qb[0:64, :].rearrange("p (c t) -> p c t", t=T)[:, g4 * 4:(g4 + 1) * 4, :]
                        x2s = qb[64:128, :].rearrange("p (c t) -> p c t", t=T)[:, g4 * 4:(g4 + 1) * 4, :]
                        x2v = x2c[:, :].rearrange("p (c t) -> p c t", t=T)
                        o1 = o1t[:, :].rearrange("p (c t) -> p c t", t=T)
                        tm = rtm[:, :].rearrange("p (c t) -> p c t", t=T)
                        cb4 = cosT[:, :].rearrange("p t -> p () t").broadcast_to([64, 4, T])
                        sb4 = sinT[:, :].rearrange("p t -> p () t").broadcast_to([64, 4, T])
                        nc.vector.tensor_copy(x2v, x2s)
                        nc.vector.tensor_tensor(o1, x1, cb4, ALU.mult)
                        nc.vector.tensor_tensor(tm, x2v, sb4, ALU.mult)
                        nc.vector.tensor_sub(o1, o1, tm)
                        nc.vector.tensor_tensor(tm, x1, sb4, ALU.mult)
                        nc.vector.tensor_tensor(x2v, x2v, cb4, ALU.mult)
                        rve.inc(nc.vector.tensor_add(tm, tm, x2v))
                        ctx.wait(nc.gpsimd, rve)
                        for hh in range(4):
                            h = g4 * 4 + hh
                            rst.inc(nc.gpsimd.dma_start(
                                out=st1[h * 64:(h + 1) * 64, :],
                                in_=o1t[:, hh * T:(hh + 1) * T]), 16)
                            rst.inc(nc.gpsimd.dma_start(
                                out=st2[h * 64:(h + 1) * 64, :],
                                in_=rtm[:, hh * T:(hh + 1) * T]), 16)
                    ctx.wait(nc.gpsimd, rst)
                ctx.bar()

            qk_pass("c_q", wq_in, ctabs["cq"], ctabs["sq"], q1_st, q2_st)
            qk_pass("c_k", wk_in, ctabs["ck"], ctabs["sk"], k1_st, k2_st)

            with contextlib.ExitStack() as es8:
                vb = sb(es8, nc, "vb", [128, 16 * T], BF16)
                with contextlib.ExitStack() as es9:
                    wb0 = sb(es9, nc, "vwb0", [128, 16 * 128], BF16)
                    wb1 = sb(es9, nc, "vwb1", [128, 16 * 128], BF16)
                    ps0 = psum(es9, nc, "vps0", [128, 512])
                    ps1 = psum(es9, nc, "vps1", [128, 512])

                    def v_cb(m, f0, fw, ps, mmn):
                        nc.vector.wait_ge(ctx.sem("mmd_c_v").h, mmn)
                        return nc.vector.tensor_copy(vb[:, m * T + f0:m * T + f0 + fw],
                                                     ps[:, 0:fw])

                    emit_stream_mm(ctx, "c_v", plain_pieces(wv_in, D), D, D,
                                   lambda kc, kk, f0, fw: xlt2[:, kc * T + f0:kc * T + f0 + fw],
                                   v_cb, [wb0, wb1], [ps0, ps1], n_free=T)
                with contextlib.ExitStack() as es9:
                    tpb = [psum(es9, nc, "vtp%d" % i, [128, 512], BF16) for i in range(2)]
                    vrow = sb(es9, nc, "vrow", [128, 512], BF16)
                    tmm = ctx.sem("vtmm")
                    tve = ctx.sem("vtve")
                    tst = ctx.sem("vtst")
                    tb = 0
                    for tt in range(8):
                        for dcg in range(4):
                            tp = tpb[tb % 2]
                            if tb >= 2:
                                ctx.wait(nc.tensor, tve, tb - 1)
                            for c4 in range(4):
                                c = dcg * 4 + c4
                                inst = nc.tensor.transpose(
                                    tp[:, c4 * 128:(c4 + 1) * 128],
                                    vb[:, c * T + tt * 128:c * T + (tt + 1) * 128],
                                    identb[:, :])
                            tmm.inc(inst)
                            ctx.wait(nc.vector, tmm)
                            if tb >= 1:
                                ctx.wait(nc.vector, tst, 16 * tb)
                            tve.inc(nc.vector.tensor_copy(vrow[:, :], tp[:, :]))
                            ctx.wait(nc.gpsimd, tve)
                            tst.inc(nc.gpsimd.dma_start(
                                out=vt1_st[tt * 128:(tt + 1) * 128,
                                           dcg * 512:(dcg + 1) * 512],
                                in_=vrow[:, :]), 16)
                            tb += 1
                    ctx.wait(nc.gpsimd, tst)
                ctx.bar()

        ctx.bar()
        sto = ctx.sem("sto")
        for t in range(8):
            sto.inc(nc.gpsimd.dma_start(out=x_out[t * 128:(t + 1) * 128, :],
                                        in_=x_tiles[t][:, :]), 16)
        ctx.wait(nc.gpsimd, sto)
    return nc


def inputs_C(state):
    full = state["full"]
    ident = np.eye(128, dtype=np.float32).astype(BF)
    half = HD // 2
    fraction = 2.0 * np.arange(half, dtype=np.float32) / HD
    timescale = 10000.0 ** fraction
    maps = []
    for r in range(NC):
        pos = np.arange(r * T, (r + 1) * T, dtype=np.float32)
        sinu = pos[None, :] / timescale[:, None]
        cosv, sinv = np.cos(sinu), np.sin(sinu)
        m = {
            "x_in": np.ascontiguousarray(state["x"][r * T:(r + 1) * T]),
            "ident": ident,
            "attn_in": np.ascontiguousarray(state["attn0T"][:, r * T:(r + 1) * T]),
            "wo_in": state["wo0_bf"],
            "gate_in": state["gate_all"],
            "up_in": state["up_all"],
            "down_in": state["down_all"],
            "wq_in": state["wq1_bf"],
            "wk_in": state["wk1_bf"],
            "wv_in": state["wv1_bf"],
            "cos1q": (cosv * L1_SCALE).astype(BF),
            "sin1q": (sinv * L1_SCALE).astype(BF),
            "cos1k": cosv.astype(BF),
            "sin1k": sinv.astype(BF),
        }
        maps.append(m)
    return maps


def run_C(state):
    nc = build_C()
    res = run_bass_kernel_spmd(nc, inputs_C(state), core_ids=list(range(NC)))
    return res.results



def build_E(parts=("ao", "ffn"), dbg=()):
    nc = bass.Bass("TRN2", target_bir_lowering=False)
    ctx = Ctx(nc)
    dbg_out = {name: nc.dram_tensor(name, shape, dt, kind="ExternalOutput").ap()
               for name, shape, dt in dbg}
    x_in = nc.dram_tensor("x_in", [T, D], F32, kind="ExternalInput").ap()
    x_out = nc.dram_tensor("x_out", [T, D], F32, kind="ExternalOutput").ap()
    ident_in = nc.dram_tensor("ident", [128, 128], BF16, kind="ExternalInput").ap()
    attn_in = nc.dram_tensor("attn_in", [D, T], BF16, kind="ExternalInput").ap()
    wo_in = nc.dram_tensor("wo_in", [D, D], BF16, kind="ExternalInput").ap()
    w1q_in = nc.dram_tensor("w1q_in", [NC * D, T], BF16, kind="ExternalInput").ap()
    w2q_in = nc.dram_tensor("w2q_in", [NC * T, D], BF16, kind="ExternalInput").ap()
    b1v = nc.dram_tensor("b1v", [128, 64], F32, kind="ExternalInput").ap()
    b2r = nc.dram_tensor("b2r", [1, D], BF16, kind="ExternalInput").ap()

    with contextlib.ExitStack() as es:
        x_tiles = [sb(es, nc, "x%d" % t, [128, D], F32) for t in range(8)]
        identb = sb(es, nc, "identb", [128, 128], BF16)
        es.enter_context(nc.Block())
        ld0 = ctx.sem("ld0")
        for t in range(8):
            ld0.inc(nc.sync.dma_start(out=x_tiles[t][:, :],
                                      in_=x_in[t * 128:(t + 1) * 128, :]), 16)
        ld0.inc(nc.sync.dma_start(out=identb[:, :], in_=ident_in), 16)

        with contextlib.ExitStack() as es2:
            atT = sb(es2, nc, "atT", [128, 16 * T], BF16)
            woS = sb(es2, nc, "woT", [128, 16 * D], BF16)
            for c in range(16):
                ld0.inc(nc.sync.dma_start(out=atT[:, c * T:(c + 1) * T],
                                          in_=attn_in[c * 128:(c + 1) * 128, :]), 16)
                ld0.inc(nc.sync.dma_start(out=woS[:, c * D:(c + 1) * D],
                                          in_=wo_in[c * 128:(c + 1) * 128, :]), 16)
            ctx.wait(nc.tensor, ld0)
            ctx.bar()
            if "ao" in parts:
                emit_token_proj(ctx, "e_ao",
                                lambda kc, tg: atT[:, kc * T + tg * 128:kc * T + (tg + 1) * 128],
                                16, [woS[:, c * D:(c + 1) * D] for c in range(16)],
                                x_tiles)
        if "dbg_x_ao" in dbg_out:
            dsem = ctx.sem("dbg1")
            for t in range(8):
                dsem.inc(nc.gpsimd.dma_start(out=dbg_out["dbg_x_ao"][t * 128:(t + 1) * 128, :],
                                             in_=x_tiles[t][:, :]), 16)
            ctx.wait(nc.gpsimd, dsem)
            ctx.bar()

        ld = ctx.sem("f1ld")
        with contextlib.ExitStack() as es3:
            xlt = sb(es3, nc, "xlt", [128, 16 * T], BF16)
            b1sb = sb(es3, nc, "b1sb", [128, 64], F32)
            b2sb = sb(es3, nc, "b2sb", [1, D], BF16)
            ones1 = sb(es3, nc, "ones1", [1, 128], BF16)
            ld.inc(nc.sync.dma_start(out=b1sb[:, :], in_=b1v), 16)
            ld.inc(nc.gpsimd.dma_start(out=b2sb[:, :], in_=b2r), 16)
            nc.vector.memset(ones1[:, :], 1.0)
            emit_ln(ctx, x_tiles, xlt, identb, "e_ln")
            ctx.wait(nc.scalar, ld)
            ctx.wait(nc.tensor, ld)
            if "dbg_xlt" in dbg_out:
                dsem2 = ctx.sem("dbg2")
                dsem2.inc(nc.gpsimd.dma_start(out=dbg_out["dbg_xlt"], in_=xlt[:, :]), 16)
                ctx.wait(nc.gpsimd, dsem2)
                ctx.bar()
            if "ffn" not in parts:
                ctx.sem("skip")
            hT_dump = "dbg_hT" in dbg_out

            for th in range(2 if "ffn" in parts else 0):
                tq0 = th * 512
                with contextlib.ExitStack() as es4:
                    hT = sb(es4, nc, "hT%d" % th, [128, 64 * 512], BF16)
                    tag = "eh%d" % th
                    with contextlib.ExitStack() as es5:
                        w1b0 = sb(es5, nc, "w1b0_%d" % th, [128, 16 * 128], BF16)
                        w1b1 = sb(es5, nc, "w1b1_%d" % th, [128, 16 * 128], BF16)
                        fps0 = psum(es5, nc, "fps0_%d" % th, [128, 512])
                        fps1 = psum(es5, nc, "fps1_%d" % th, [128, 512])
                        def w1_pieces(m):
                            r, fl = divmod(m, 8)
                            return [(w1q_in[r * D + kc * 128:r * D + (kc + 1) * 128,
                                            fl * 128:(fl + 1) * 128], 0, 128, kc * 128, 128)
                                    for kc in range(16)]

                        def rhs_fn(kc, kk, f0, fw, _tq0=tq0):
                            return xlt[:, kc * T + _tq0 + f0:kc * T + _tq0 + f0 + fw]

                        def out_cb(m, f0, fw, ps, mmn, _tag=tag):
                            nc.scalar.wait_ge(ctx.sem("mmd_" + _tag).h, mmn)
                            return nc.scalar.activation(hT[:, m * 512:m * 512 + fw],
                                                        ps[:, 0:fw], AF.Gelu,
                                                        bias=b1sb[:, m:m + 1])

                        emit_stream_mm(ctx, tag, w1_pieces, D, DFF, rhs_fn, out_cb,
                                       [w1b0, w1b1], [fps0, fps1], n_free=512)
                    if th == 0 and hT_dump:
                        dsem3 = ctx.sem("dbg3")
                        dsem3.inc(nc.gpsimd.dma_start(out=dbg_out["dbg_hT"], in_=hT[:, :]), 16)
                        ctx.wait(nc.gpsimd, dsem3)
                        ctx.bar()

                    tag2 = "eo%d" % th
                    w2ld = ctx.sem("w2ld_" + tag2)
                    w2fr = ctx.sem("w2fr_" + tag2)
                    mm2 = ctx.sem("mm2_" + tag2)
                    ve2 = ctx.sem("ve2_" + tag2)
                    with contextlib.ExitStack() as es6:
                        w2bufs4 = [sb(es6, nc, "w2b%d_%d" % (i, th), [128, D], BF16)
                                   for i in range(4)]
                        ops = [psum(es6, nc, "ops%d_%d" % (i, th), [128, 512]) for i in range(8)]

                        def w2_load(f):
                            r, fl = divmod(f, 8)
                            w2ld.inc(nc.sync.dma_start(
                                out=w2bufs4[f % 2][:, :],
                                in_=w2q_in[r * T + fl * 128:r * T + (fl + 1) * 128, :]), 16)

                        for dch in range(2):
                            ctx.wait(nc.sync, w2fr)
                            w2_load(0)
                            w2_load(1)
                            nld = 2
                            for f in range(64):
                                ctx.wait(nc.tensor, w2ld, 16 * (f + 1) + 16 * 64 * dch)
                                for tg in range(4):
                                    for d2 in range(2):
                                        inst = nc.tensor.matmul(
                                            ops[tg * 2 + d2][:, :],
                                            hT[:, f * 512 + tg * 128:f * 512 + (tg + 1) * 128],
                                            w2bufs4[f % 2][:, dch * 1024 + d2 * 512:
                                                           dch * 1024 + (d2 + 1) * 512],
                                            start=(f == 0), stop=False)
                                w2fr.inc(inst)
                                if nld < 64:
                                    ctx.wait(nc.sync, w2fr, w2fr.n)
                                    w2_load(nld)
                                    nld += 1
                            for tg in range(4):
                                for d2 in range(2):
                                    inst = nc.tensor.matmul(
                                        ops[tg * 2 + d2][:, :], ones1[:, :],
                                        b2sb[:, dch * 1024 + d2 * 512:dch * 1024 + (d2 + 1) * 512],
                                        start=False, stop=True)
                            mm2.inc(inst)
                            ctx.wait(nc.vector, mm2)
                            for tg in range(4):
                                xt = x_tiles[th * 4 + tg]
                                for d2 in range(2):
                                    inst = nc.vector.scalar_tensor_tensor(
                                        xt[:, dch * 1024 + d2 * 512:dch * 1024 + (d2 + 1) * 512],
                                        ops[tg * 2 + d2][:, :], 1.0,
                                        xt[:, dch * 1024 + d2 * 512:dch * 1024 + (d2 + 1) * 512],
                                        ALU.mult, ALU.add)
                            ve2.inc(inst)
                            ctx.wait(nc.tensor, ve2)
                        ctx.bar()

        ctx.bar()
        sto = ctx.sem("sto")
        for t in range(8):
            sto.inc(nc.gpsimd.dma_start(out=x_out[t * 128:(t + 1) * 128, :],
                                        in_=x_tiles[t][:, :]), 16)
        ctx.wait(nc.gpsimd, sto)
    return nc


def inputs_E(state):
    ident = np.eye(128, dtype=np.float32).astype(BF)
    full = state["full"]
    maps = []
    for r in range(NC):
        maps.append({
            "x_in": np.ascontiguousarray(state["x"][r * T:(r + 1) * T]),
            "ident": ident,
            "attn_in": np.ascontiguousarray(state["attn1T"][:, r * T:(r + 1) * T]),
            "wo_in": state["wo1_bf"],
            "w1q_in": state["w1q"],
            "w2q_in": state["w2q"],
            "b1v": np.ascontiguousarray(np.asarray(full["L1_ffn_b1"], np.float32).reshape(64, 128).T),
            "b2r": np.asarray(full["L1_ffn_b2"], np.float32).reshape(1, D).astype(BF),
        })
    return maps


def run_E(state):
    nc = build_E()
    res = run_bass_kernel_spmd(nc, inputs_E(state), core_ids=list(range(NC)))
    return np.concatenate([res.results[r]["x_out"] for r in range(NC)], axis=0)


def kernel(**inputs):
    full = inputs
    x = np.ascontiguousarray(np.asarray(full["L0_attn_x"], np.float32).reshape(S, D))
    state = {"full": full, "x": x}

    resA = run_A(state)
    cat = np.concatenate
    state["qn_full"] = cat([resA[r]["qn_st"] for r in range(NC)], axis=1)
    state["qr_full"] = cat([resA[r]["qr_st"] for r in range(NC)], axis=1)
    state["kn_full"] = cat([resA[r]["kn_st"] for r in range(NC)], axis=1)
    state["vt_full"] = cat([resA[r]["vt_st"] for r in range(NC)], axis=0)
    state["kpe_full"] = cat([resA[r]["kpe_st"] for r in range(NC)], axis=1)
    state["absums"] = np.asarray(
        [[resA[r]["absum"][0, 0] for r in range(NC)] +
         [resA[r]["absum"][0, 1] for r in range(NC)]], np.float32)
    state["gate_all"] = np.ascontiguousarray(cat([resA[r]["gate_T"] for r in range(NC)], axis=1))
    state["up_all"] = np.ascontiguousarray(cat([resA[r]["up_T"] for r in range(NC)], axis=1))
    state["down_all"] = np.ascontiguousarray(cat([resA[r]["down_T"] for r in range(NC)], axis=0))
    for nm in ("wo0", "wq1", "wk1", "wv1", "wo1"):
        state[nm + "_bf"] = np.ascontiguousarray(cat([resA[r][nm + "_bf"] for r in range(NC)], axis=0))

    resB = run_B(state)
    state["attn0T"] = np.ascontiguousarray(cat([resB[r]["attn_st"] for r in range(NC)], axis=0))
    state["w1q"] = np.ascontiguousarray(cat([resB[r]["w1q_st"] for r in range(NC)], axis=0))
    state["w2q"] = np.ascontiguousarray(cat([resB[r]["w2q_st"] for r in range(NC)], axis=0))

    resC = run_C(state)
    state["x"] = np.ascontiguousarray(cat([resC[r]["x_out"] for r in range(NC)], axis=0))
    state["q1_full"] = cat([resC[r]["q1_st"] for r in range(NC)], axis=1)
    state["q2_full"] = cat([resC[r]["q2_st"] for r in range(NC)], axis=1)
    state["k1_full"] = cat([resC[r]["k1_st"] for r in range(NC)], axis=1)
    state["k2_full"] = cat([resC[r]["k2_st"] for r in range(NC)], axis=1)
    state["vt1_full"] = cat([resC[r]["vt1_st"] for r in range(NC)], axis=0)

    resD = run_D(state)
    state["attn1T"] = np.ascontiguousarray(cat([resD[r]["attn_st"] for r in range(NC)], axis=0))
    state["wo1_bf"] = state["wo1_bf"]

    x = run_E(state)
    return x.reshape(1, S, D).astype(np.float32)
